# version 05
# baseline (speedup 1.0000x reference)
import os
import numpy as np
import ml_dtypes
import concourse.bass as bass
import concourse.mybir as mybir
from concourse.bass_utils import run_bass_kernel_spmd

F32 = mybir.dt.float32
BF16 = mybir.dt.bfloat16
U8 = mybir.dt.uint8
AF = mybir.ActivationFunctionType
AX = mybir.AxisListType
OP = mybir.AluOpType

D = 4096
KC = 32
INTER = 11008
NIC = 43
T_A = 1536
T_OWN = 1024
HALO = 512
NQ, NKV, HD, WIN = 32, 8, 128, 512
QK_SCALE = float(1.0 / np.sqrt(128.0))
NEG = -1.0e30
KB = 1024
f32c = np.float32
SEM_NAMES = {}


def _carve(arena, off, shape, dtype):
    nb = int(np.prod(shape[1:])) * mybir.dt.size(dtype)
    assert off % 32 == 0
    v = arena[:, off:off + nb].bitcast(dtype)
    if len(shape) > 2:
        dims = " ".join(f"d{i}" for i in range(1, len(shape)))
        v = v.rearrange(f"p ({dims}) -> p {dims}",
                        **{f"d{i}": shape[i] for i in range(1, len(shape))})
    return v, off + ((nb + 31) // 32) * 32


class Ctr:
    def __init__(self, h):
        self.h = h
        self.n = 0

    def inc(self, ins, k=1):
        ins.then_inc(self.h, k)
        self.n += k
        return self.n


def build_nc(phases=("A", "B", "C"), expose_ya=False, expose_yb=False,
             expose_q=False, expose_o=False):
    nc = bass.Bass("TRN2", target_bir_lowering=False)

    x_in = nc.dram_tensor("x_in", [T_OWN, D], F32, kind="ExternalInput").ap()
    wg_t = nc.dram_tensor("wg_t", [NIC, 128, KC, 256], F32, kind="ExternalInput").ap()
    wu_t = nc.dram_tensor("wu_t", [NIC, 128, KC, 256], F32, kind="ExternalInput").ap()
    wd_t = nc.dram_tensor("wd_t", [8, 128, 86, 512], F32, kind="ExternalInput").ap()
    wk_t = nc.dram_tensor("wk_t", [NKV, 128, KC, 128], F32, kind="ExternalInput").ap()
    wv_t = nc.dram_tensor("wv_t", [NKV, 128, KC, 128], F32, kind="ExternalInput").ap()
    wq_t = nc.dram_tensor("wq_t", [NQ, 128, KC, 128], F32, kind="ExternalInput").ap()
    wo_t = nc.dram_tensor("wo_t", [16, 128, KC, 256], F32, kind="ExternalInput").ap()
    mask_in = nc.dram_tensor("mask_in", [8, 128, 640], F32, kind="ExternalInput").ap()
    ident_in = nc.dram_tensor("ident_in", [128, 128], F32, kind="ExternalInput").ap()

    out = nc.dram_tensor("out", [T_OWN, D], F32, kind="ExternalOutput").ap()
    ya = nc.dram_tensor("ya", [T_A, D], F32,
                        kind="ExternalOutput" if expose_ya else "Internal").ap()
    yb = nc.dram_tensor("yb", [T_OWN, D], F32,
                        kind="ExternalOutput" if expose_yb else "Internal").ap()
    hbuf = nc.dram_tensor("hbuf", [86, 128, T_A], BF16, kind="Internal").ap()
    cc_in = nc.dram_tensor("cc_in", [HALO, D], F32, kind="Internal").ap()
    cc_out = nc.dram_tensor("cc_out", [8 * HALO, D], F32, kind="Internal",
                        addr_space="Shared").ap()
    qbuf = nc.dram_tensor("qbuf", [NQ, 128, T_OWN], BF16,
                          kind="ExternalOutput" if expose_q else "Internal").ap()
    obuf = (nc.dram_tensor("obuf", [NQ, 128, T_OWN], BF16,
                           kind="ExternalOutput").ap() if expose_o else None)

    from contextlib import ExitStack
    ctx = ExitStack()
    with ctx:
        arena = ctx.enter_context(nc.sbuf_tensor("arena", [128, 184 * KB], U8)).ap()
        bank = [ctx.enter_context(nc.psum_tensor(f"bank{i}", [128, 512], F32)).ap()
                for i in range(8)]

        o = 0
        idb, o = _carve(arena, o, [128, 128], BF16)
        idf, o = _carve(arena, o, [128, 128], F32)
        stats, o = _carve(arena, o, [128, 32], F32)
        assert o <= 1 * KB
        RAWS = stats[:, 0:1]
        MEAN = stats[:, 1:2]
        SUMSQ = stats[:, 2:3]
        VAR = stats[:, 3:4]
        M2 = stats[:, 4:5]
        RSTD_S = stats[:, 5:6]
        RSTD = stats[:, 6:7]
        MNEG = [stats[:, 8 + 4 * i:9 + 4 * i] for i in range(4)]
        SSUM = [stats[:, 9 + 4 * i:10 + 4 * i] for i in range(4)]
        RINV = [stats[:, 10 + 4 * i:11 + 4 * i] for i in range(4)]

        xn_ab, _ = _carve(arena, 1 * KB, [128, KC, T_A], BF16)
        xn_c, _ = _carve(arena, 1 * KB, [128, KC, T_OWN], BF16)
        wd_cache, _ = _carve(arena, 1 * KB, [128, 86, 512], BF16)
        o2 = 1 * KB
        qg_b = []
        for i in range(2):
            v, o2 = _carve(arena, o2, [128, 4, T_OWN], BF16)
            qg_b.append(v)
        obuf_res, o2 = _carve(arena, o2, [128, NQ, T_OWN], BF16)
        s_sb, p_sb, pn_sb = [], [], []
        for i in range(4):
            v, o2 = _carve(arena, o2, [128, 640], F32)
            s_sb.append(v)
        for i in range(4):
            v, o2 = _carve(arena, o2, [128, 640], BF16)
            pn_sb.append(v)
        assert o2 <= 97 * KB, o2
        o2w = 145 * KB
        for i in range(4):
            v, o2w = _carve(arena, o2w, [128, 640], F32)
            p_sb.append(v)
        pt_sb, o2w = _carve(arena, o2w, [128, 5, 4, 128], BF16)
        assert o2w <= 161 * KB, o2w
        mask_sb, _ = _carve(arena, 161 * KB, [128, 8, 640], F32)
        o3 = 97 * KB
        wo_b = []
        for i in range(2):
            v, o3 = _carve(arena, o3, [128, KC, 256], BF16)
            wo_b.append(v)
        y3_b = []
        for i in range(4):
            v, o3 = _carve(arena, o3, [128, 2, 512], F32)
            y3_b.append(v)
        r3_b = []
        for i in range(6):
            v, o3 = _carve(arena, o3, [128, 4, 256], F32)
            r3_b.append(v)
        os3_b = []
        for i in range(2):
            v, o3 = _carve(arena, o3, [128, 4, 256], F32)
            os3_b.append(v)
        assert o3 <= 184 * KB, o3

        W1 = 97 * KB
        o4 = W1
        wgb, wub = [], []
        for i in range(2):
            v, o4 = _carve(arena, o4, [128, KC, 256], BF16)
            wgb.append(v)
        for i in range(2):
            v, o4 = _carve(arena, o4, [128, KC, 256], BF16)
            wub.append(v)
        assert o4 <= 161 * KB
        o5 = W1
        trash, o5 = _carve(arena, o5, [128, D], F32)
        xt_b, xnb_b = [], []
        for i in range(2):
            v, o5 = _carve(arena, o5, [128, D], F32)
            xt_b.append(v)
        for i in range(2):
            v, o5 = _carve(arena, o5, [128, D], BF16)
            xnb_b.append(v)
        assert o5 <= 161 * KB
        o6 = W1
        hch_b, y2_b, r2_b, os2_b = [], [], [], []
        for lst, shp in ((hch_b, None), (y2_b, [128, 4, 512]), (r2_b, [128, 4, 512]),
                         (os2_b, [128, 4, 512])):
            for i in range(2):
                if lst is hch_b:
                    v, o6 = _carve(arena, o6, [128, 8, 512], BF16)
                else:
                    v, o6 = _carve(arena, o6, shp, F32)
                lst.append(v)
        assert o6 <= 161 * KB
        o7 = W1
        k_sb, o7 = _carve(arena, o7, [128, NKV, T_A], BF16)
        v_tok, o7 = _carve(arena, o7, [128, NKV, 12, 128], BF16)
        wpj_b = []
        for i in range(2):
            v, o7 = _carve(arena, o7, [128, KC, 128], BF16)
            wpj_b.append(v)
        assert o7 <= 161 * KB
        o9 = 161 * KB
        vf_sb, o9 = _carve(arena, o9, [128, T_A], BF16)
        qs_b = []
        for i in range(2):
            v, o9 = _carve(arena, o9, [128, 512], BF16)
            qs_b.append(v)
        gs_sb, o9 = _carve(arena, o9, [128, 3, 512], F32)
        hs_b = []
        for i in range(2):
            v, o9 = _carve(arena, o9, [128, 3, 512], BF16)
            hs_b.append(v)
        assert o9 <= 180 * KB, o9

        block = ctx.enter_context(nc.Block())

        s_wl = Ctr(ctx.enter_context(nc.semaphore("s_wl")))
        s_hl = Ctr(ctx.enter_context(nc.semaphore("s_hl")))
        s_st = Ctr(ctx.enter_context(nc.semaphore("s_st")))
        s_pe = Ctr(ctx.enter_context(nc.semaphore("s_pe")))
        s_ac = Ctr(ctx.enter_context(nc.semaphore("s_ac")))
        s_dv = Ctr(ctx.enter_context(nc.semaphore("s_dv")))
        ALL = [s_wl, s_hl, s_st, s_pe, s_ac, s_dv]
        global SEM_NAMES
        SEM_NAMES = {c.h.num: n for c, n in zip(ALL, ["wl", "hl", "st", "pe", "ac", "dv"])}

        def join_all():
            snap = [(c.h, c.n) for c in ALL]

            def wait_all(e):
                for h, n in snap:
                    if n:
                        e.wait_ge(h, n)
            block.sync(wait_all)
            block.gpsimd(wait_all)
            block.tensor(wait_all)
            block.scalar(wait_all)
            block.vector(wait_all)

        def p_pre_g(g):
            g.dma_start(idb[:], ident_in).then_inc(s_wl.h, 16)
        block.gpsimd(p_pre_g)
        s_wl.n += 16

        def p_pre_s(sy):
            sy.dma_start(idf[:], ident_in).then_inc(s_hl.h, 16)
        block.sync(p_pre_s)
        s_hl.n += 16
        join_all()

        def ln_phase(src, n_tiles, xn):
            hl0, ac0, dv0, pe0 = s_hl.n, s_ac.n, s_dv.n, s_pe.n
            DVP, ACP, PEP = 6, 6, 4

            def f_sync(sy):
                for tt in range(n_tiles):
                    if tt >= 2:
                        sy.wait_ge(s_dv.h, dv0 + DVP * (tt - 2) + 6)
                        sy.wait_ge(s_ac.h, ac0 + ACP * (tt - 2) + 1)
                    sy.dma_start(xt_b[tt % 2][:], src[tt * 128:(tt + 1) * 128, :]
                                 ).then_inc(s_hl.h, 16)
            block.sync(f_sync)
            s_hl.n += 16 * n_tiles

            def f_dve(v):
                for tt in range(n_tiles):
                    x_t = xt_b[tt % 2]
                    v.wait_ge(s_hl.h, hl0 + 16 * (tt + 1))
                    if tt >= 1:
                        v.wait_ge(s_dv.h, dv0 + DVP * tt)
                    ins = v.reduce_sum(RAWS, x_t[:], axis=AX.X)
                    ins.then_inc(s_dv.h, 1)
                    v.wait_ge(s_dv.h, dv0 + DVP * tt + 1)
                    ins = v.tensor_scalar_mul(MEAN, RAWS, 1.0 / D)
                    ins.then_inc(s_dv.h, 1)
                    v.wait_ge(s_ac.h, ac0 + ACP * tt + 1)
                    v.tensor_scalar(VAR, SUMSQ, 1.0 / D, None, OP.mult)
                    v.wait_ge(s_dv.h, dv0 + DVP * tt + 2)
                    ins = v.tensor_mul(M2, MEAN, MEAN)
                    ins.then_inc(s_dv.h, 1)
                    v.wait_ge(s_dv.h, dv0 + DVP * tt + 3)
                    ins = v.tensor_tensor(VAR, VAR, M2, OP.subtract)
                    ins.then_inc(s_dv.h, 1)
                    v.wait_ge(s_ac.h, ac0 + ACP * tt + 2)
                    ins = v.reciprocal(RSTD, RSTD_S)
                    ins.then_inc(s_dv.h, 1)
                    if tt >= 2:
                        v.wait_ge(s_pe.h, pe0 + PEP * (tt - 2) + 4)
                    v.wait_ge(s_dv.h, dv0 + DVP * tt + 5)
                    ins = v.tensor_scalar(xnb_b[tt % 2][:], x_t[:], MEAN, RSTD,
                                          OP.subtract, OP.mult)
                    ins.then_inc(s_dv.h, 1)
            block.vector(f_dve)
            s_dv.n += DVP * n_tiles

            def f_act(sc):
                for tt in range(n_tiles):
                    x_t = xt_b[tt % 2]
                    sc.wait_ge(s_hl.h, hl0 + 16 * (tt + 1))
                    if tt >= 1:
                        sc.wait_ge(s_dv.h, dv0 + DVP * (tt - 1) + 4)
                    ins = sc.activation(trash[:], x_t[:], AF.Square,
                                        bias=0.0, scale=1.0, accum_out=SUMSQ)
                    ins.then_inc(s_ac.h, 1)
                    sc.wait_ge(s_dv.h, dv0 + DVP * tt + 4)
                    ins = sc.activation(RSTD_S, VAR, AF.Sqrt, bias=f32c(1e-5),
                                        scale=1.0)
                    ins.then_inc(s_ac.h, 1)
                    for b in range(4):
                        sc.wait_ge(s_pe.h, pe0 + PEP * tt + b + 1)
                        src_ps = bank[6 + b % 2][:].bitcast(BF16)[:, 0:1024].rearrange(
                            "p (a c) -> p a c", a=8)
                        ins = sc.copy(xn[:, b * 8:(b + 1) * 8, tt * 128:(tt + 1) * 128],
                                      src_ps)
                        ins.then_inc(s_ac.h, 1)
            block.scalar(f_act)
            s_ac.n += ACP * n_tiles

            def f_pe(t):
                for tt in range(n_tiles):
                    t.wait_ge(s_dv.h, dv0 + DVP * tt + 6)
                    for b in range(4):
                        gb = tt * 4 + b
                        if gb >= 2:
                            tp, bp = divmod(gb - 2, 4)
                            t.wait_ge(s_ac.h, ac0 + ACP * tp + 3 + bp)
                        dst = bank[6 + b % 2][:].bitcast(BF16)
                        for j in range(8):
                            kc = b * 8 + j
                            ins = t.transpose(dst[:, j * 128:(j + 1) * 128],
                                              xnb_b[tt % 2][:, kc * 128:(kc + 1) * 128],
                                              idb[:])
                        ins.then_inc(s_pe.h, 1)
            block.tensor(f_pe)
            s_pe.n += PEP * n_tiles

        def ffn_up(xn, n_tc):
            wl0, pe0, ac0, dv0, st0 = s_wl.n, s_pe.n, s_ac.n, s_dv.n, s_st.n
            NG = 2 * NIC

            def f_g(g):
                for ic in range(NIC):
                    if ic >= 2:
                        g.wait_ge(s_pe.h, pe0 + 4 * ic - 4)
                    g.dma_start(wgb[ic % 2][:], wg_t[ic]).then_inc(s_wl.h, 16)
                    g.dma_start(wub[ic % 2][:], wu_t[ic]).then_inc(s_wl.h, 16)
            block.gpsimd(f_g)
            s_wl.n += 32 * NIC

            def f_pe(t):
                for ic in range(NIC):
                    t.wait_ge(s_wl.h, wl0 + 32 * (ic + 1))
                    for i2 in range(2):
                        gi = 2 * ic + i2
                        if gi >= 1:
                            t.wait_ge(s_ac.h, ac0 + gi)
                        for kc in range(KC):
                            for tc in range(n_tc):
                                ins = t.matmul(bank[tc][:],
                                               wgb[ic % 2][:, kc, i2 * 128:(i2 + 1) * 128],
                                               xn[:, kc, tc * 512:(tc + 1) * 512],
                                               start=(kc == 0), stop=(kc == KC - 1))
                        ins.then_inc(s_pe.h, 1)
                        if gi >= 1:
                            t.wait_ge(s_dv.h, dv0 + gi)
                        for kc in range(KC):
                            for tc in range(n_tc):
                                ins = t.matmul(bank[3 + tc][:],
                                               wub[ic % 2][:, kc, i2 * 128:(i2 + 1) * 128],
                                               xn[:, kc, tc * 512:(tc + 1) * 512],
                                               start=(kc == 0), stop=(kc == KC - 1))
                        ins.then_inc(s_pe.h, 1)
            block.tensor(f_pe)
            s_pe.n += 2 * NG

            def f_act(sc):
                for gi in range(NG):
                    sc.wait_ge(s_pe.h, pe0 + 2 * gi + 1)
                    if gi >= 1:
                        sc.wait_ge(s_dv.h, dv0 + gi)
                    for tc in range(n_tc):
                        ins = sc.activation(gs_sb[:, tc, :], bank[tc][:], AF.Silu,
                                            bias=0.0, scale=1.0)
                    ins.then_inc(s_ac.h, 1)
            block.scalar(f_act)
            s_ac.n += NG

            def f_dve(v):
                for gi in range(NG):
                    v.wait_ge(s_pe.h, pe0 + 2 * gi + 2)
                    v.wait_ge(s_ac.h, ac0 + gi + 1)
                    if gi >= 2:
                        v.wait_ge(s_st.h, st0 + 16 * (gi - 1))
                    for tc in range(n_tc):
                        ins = v.scalar_tensor_tensor(hs_b[gi % 2][:, tc, :],
                                                     bank[3 + tc][:], 0.5,
                                                     gs_sb[:, tc, :],
                                                     OP.mult, OP.mult)
                    ins.then_inc(s_dv.h, 1)
            block.vector(f_dve)
            s_dv.n += NG

            def f_store(sy):
                for gi in range(NG):
                    sy.wait_ge(s_dv.h, dv0 + gi + 1)
                    sy.dma_start(hbuf[gi, :, 0:n_tc * 512],
                                 hs_b[gi % 2][:, 0:n_tc, :].rearrange("p a c -> p (a c)")
                                 ).then_inc(s_st.h, 16)
            block.sync(f_store)
            s_st.n += 16 * NG

        def ffn_down(res_src, dst, n_tc):
            wl0, pe0, ac0, dv0, hl0, st0 = (s_wl.n, s_pe.n, s_ac.n, s_dv.n,
                                            s_hl.n, s_st.n)
            NT = 8 * n_tc
            NB = 11
            PEP = NB + 4

            def pe_mm(g2):
                return pe0 + PEP * g2 + NB

            def pe_batch(gl):
                return pe0 + PEP * (gl // NB) + (gl % NB) + 1

            def pe_tr(g2, tt):
                return pe0 + PEP * g2 + NB + tt + 1

            def f_g(g):
                for dmc in range(8):
                    if dmc >= 1:
                        for g2 in range((dmc - 1) * n_tc, dmc * n_tc):
                            g.wait_ge(s_dv.h, dv0 + 4 * (g2 + 1))
                            g.dma_start(
                                dst[(g2 % n_tc) * 512:((g2 % n_tc) + 1) * 512,
                                    (g2 // n_tc) * 512:((g2 // n_tc) + 1) * 512]
                                .rearrange("(a p) c -> p a c", p=128),
                                os2_b[g2 % 2][:]).then_inc(s_st.h, 16)
                        g.wait_ge(s_pe.h, pe_mm(dmc * n_tc - 1))
                    g.dma_start(wd_cache[:], wd_t[dmc]).then_inc(s_wl.h, 16)
                for g2 in range(7 * n_tc, 8 * n_tc):
                    g.wait_ge(s_dv.h, dv0 + 4 * (g2 + 1))
                    g.dma_start(
                        dst[(g2 % n_tc) * 512:((g2 % n_tc) + 1) * 512,
                            (g2 // n_tc) * 512:((g2 // n_tc) + 1) * 512]
                        .rearrange("(a p) c -> p a c", p=128),
                        os2_b[g2 % 2][:]).then_inc(s_st.h, 16)
            block.gpsimd(f_g)
            s_wl.n += 16 * 8
            s_st.n += 16 * NT

            def f_load(sy):
                for g2 in range(NT):
                    dmc, tc = divmod(g2, n_tc)
                    if g2 >= 2:
                        sy.wait_ge(s_dv.h, dv0 + 4 * (g2 - 1))
                    sy.dma_start(
                        r2_b[g2 % 2][:],
                        res_src[tc * 512:(tc + 1) * 512, dmc * 512:(dmc + 1) * 512]
                        .rearrange("(a p) c -> p a c", p=128)).then_inc(s_hl.h, 16)
                    for j in range(NB):
                        gl = g2 * NB + j
                        if gl >= 2:
                            sy.wait_ge(s_pe.h, pe_batch(gl - 2))
                        nic8 = 8 if j < 10 else 6
                        sy.dma_start(
                            hch_b[gl % 2][:, 0:nic8, :],
                            hbuf[j * 8:j * 8 + nic8, :, tc * 512:(tc + 1) * 512]
                            .rearrange("a p c -> p a c")).then_inc(s_hl.h, 16)
            block.sync(f_load)
            s_hl.n += 16 * NT * (NB + 1)

            def f_pe(t):
                for g2 in range(NT):
                    dmc, tc = divmod(g2, n_tc)
                    bs = 4 * (g2 % 2)
                    t.wait_ge(s_wl.h, wl0 + 16 * (dmc + 1))
                    if g2 >= 2:
                        t.wait_ge(s_dv.h, dv0 + 4 * (g2 - 1))
                    for j in range(NB):
                        t.wait_ge(s_hl.h, hl0 + 16 * (g2 * (NB + 1) + j + 2))
                        nic8 = 8 if j < 10 else 6
                        gl = g2 * NB + j
                        for i8 in range(nic8):
                            ic = j * 8 + i8
                            for dm2 in range(4):
                                ins = t.matmul(bank[bs + dm2][:],
                                               wd_cache[:, ic, dm2 * 128:(dm2 + 1) * 128],
                                               hch_b[gl % 2][:, i8, :],
                                               start=(ic == 0), stop=(ic == 85))
                        ins.then_inc(s_pe.h, 1)
                    t.wait_ge(s_ac.h, ac0 + g2 + 1)
                    for tt in range(4):
                        dstb = bank[bs + tt][:]
                        for dm2 in range(4):
                            ins = t.transpose(dstb[:, dm2 * 128:(dm2 + 1) * 128],
                                              y2_b[g2 % 2][:, dm2, tt * 128:(tt + 1) * 128],
                                              idf[:])
                        ins.then_inc(s_pe.h, 1)
            block.tensor(f_pe)
            s_pe.n += PEP * NT

            def f_act(sc):
                for g2 in range(NT):
                    bs = 4 * (g2 % 2)
                    sc.wait_ge(s_pe.h, pe_mm(g2))
                    if g2 >= 2:
                        sc.wait_ge(s_pe.h, pe_tr(g2 - 2, 3))
                    for dm2 in range(4):
                        ins = sc.copy(y2_b[g2 % 2][:, dm2, :], bank[bs + dm2][:])
                    ins.then_inc(s_ac.h, 1)
            block.scalar(f_act)
            s_ac.n += NT

            def f_dve(v):
                for g2 in range(NT):
                    bs = 4 * (g2 % 2)
                    v.wait_ge(s_hl.h, hl0 + 16 * (g2 * (NB + 1) + 1))
                    if g2 >= 2:
                        v.wait_ge(s_st.h, st0 + 16 * (g2 - 1))
                    for tt in range(4):
                        v.wait_ge(s_pe.h, pe_tr(g2, tt))
                        ins = v.tensor_tensor(os2_b[g2 % 2][:, tt, :],
                                              bank[bs + tt][:],
                                              r2_b[g2 % 2][:, tt, :], OP.add)
                        ins.then_inc(s_dv.h, 1)
            block.vector(f_dve)
            s_dv.n += 4 * NT

        def proj_phase(xn):
            wl0, pe0, ac0, dv0, st0 = s_wl.n, s_pe.n, s_ac.n, s_dv.n, s_st.n

            def pe_k(kh):
                return pe0 + kh + 1

            def pe_vp(kh):
                return pe0 + NKV + 2 * kh + 1

            def pe_vt(kh):
                return pe0 + NKV + 2 * kh + 2

            def pe_q(h):
                return pe0 + 3 * NKV + h + 1

            def ac_k(kh):
                return ac0 + kh + 1

            def ac_vf(kh):
                return ac0 + NKV + 3 * kh + 1

            def ac_vt(kh, i):
                return ac0 + NKV + 3 * kh + 2 + i

            def ac_q(h, tc):
                return ac0 + 4 * NKV + 2 * h + tc + 1

            def f_w(g):
                for kh in range(NKV):
                    if kh >= 2:
                        g.wait_ge(s_pe.h, pe_k(kh - 2))
                    g.dma_start(wpj_b[kh % 2][:], wk_t[kh]).then_inc(s_wl.h, 16)
                for kh in range(NKV):
                    g.wait_ge(s_pe.h, pe_k(6 + kh % 2) if kh < 2
                              else pe_vt(kh - 2))
                    g.dma_start(wpj_b[kh % 2][:], wv_t[kh]).then_inc(s_wl.h, 16)
                for h in range(NQ):
                    g.wait_ge(s_pe.h, pe_vt(6 + h % 2) if h < 2
                              else pe_q(h - 2))
                    g.dma_start(wpj_b[h % 2][:], wq_t[h]).then_inc(s_wl.h, 16)
            block.gpsimd(f_w)
            s_wl.n += 16 * (2 * NKV + NQ)

            def f_pe(t):
                for kh in range(NKV):
                    t.wait_ge(s_wl.h, wl0 + 16 * (kh + 1))
                    if kh >= 1:
                        t.wait_ge(s_ac.h, ac_k(kh - 1))
                    for kc in range(KC):
                        for tc in range(3):
                            ins = t.matmul(bank[tc][:],
                                           wpj_b[kh % 2][:, kc, :],
                                           xn[:, kc, tc * 512:(tc + 1) * 512],
                                           start=(kc == 0), stop=(kc == KC - 1))
                    ins.then_inc(s_pe.h, 1)
                for kh in range(NKV):
                    t.wait_ge(s_wl.h, wl0 + 16 * NKV + 16 * (kh + 1))
                    t.wait_ge(s_ac.h, ac_vf(kh - 1) if kh >= 1 else ac_k(NKV - 1))
                    for kc in range(KC):
                        for tc in range(3):
                            ins = t.matmul(bank[tc][:],
                                           wpj_b[kh % 2][:, kc, :],
                                           xn[:, kc, tc * 512:(tc + 1) * 512],
                                           start=(kc == 0), stop=(kc == KC - 1))
                    ins.then_inc(s_pe.h, 1)
                    t.wait_ge(s_ac.h, ac_vf(kh))
                    if kh >= 1:
                        t.wait_ge(s_ac.h, ac_vt(kh - 1, 1))
                    for tt in range(12):
                        dstb = bank[3 + tt // 8][:].bitcast(BF16)
                        ins = t.transpose(dstb[:, (tt % 8) * 128:(tt % 8 + 1) * 128],
                                          vf_sb[:, tt * 128:(tt + 1) * 128], idb[:])
                    ins.then_inc(s_pe.h, 1)
                for h in range(NQ):
                    t.wait_ge(s_wl.h, wl0 + 32 * NKV + 16 * (h + 1))
                    if h >= 1:
                        t.wait_ge(s_ac.h, ac_q(h - 1, 1))
                    for kc in range(KC):
                        for tc in range(2):
                            ins = t.matmul(bank[5 + tc][:],
                                           wpj_b[h % 2][:, kc, :],
                                           xn[:, kc, HALO + tc * 512:HALO + (tc + 1) * 512],
                                           start=(kc == 0), stop=(kc == KC - 1))
                    ins.then_inc(s_pe.h, 1)
            block.tensor(f_pe)
            s_pe.n += 3 * NKV + NQ

            def f_act(sc):
                for kh in range(NKV):
                    sc.wait_ge(s_pe.h, pe_k(kh))
                    for tc in range(3):
                        ins = sc.copy(k_sb[:, kh, tc * 512:(tc + 1) * 512],
                                      bank[tc][:])
                    ins.then_inc(s_ac.h, 1)
                for kh in range(NKV):
                    sc.wait_ge(s_pe.h, pe_vp(kh))
                    if kh >= 1:
                        sc.wait_ge(s_pe.h, pe_vt(kh - 1))
                    for tc in range(3):
                        ins = sc.copy(vf_sb[:, tc * 512:(tc + 1) * 512], bank[tc][:])
                    ins.then_inc(s_ac.h, 1)
                    sc.wait_ge(s_pe.h, pe_vt(kh))
                    ins = sc.copy(v_tok[:, kh, 0:8, :],
                                  bank[3][:].bitcast(BF16)[:, 0:1024].rearrange(
                                      "p (a c) -> p a c", a=8))
                    ins.then_inc(s_ac.h, 1)
                    ins = sc.copy(v_tok[:, kh, 8:12, :],
                                  bank[4][:].bitcast(BF16)[:, 0:512].rearrange(
                                      "p (a c) -> p a c", a=4))
                    ins.then_inc(s_ac.h, 1)
                for h in range(NQ):
                    sc.wait_ge(s_pe.h, pe_q(h))
                    for tc in range(2):
                        qi = 2 * h + tc
                        if qi >= 2:
                            sc.wait_ge(s_st.h, st0 + 16 * (qi - 1))
                        ins = sc.activation(qs_b[qi % 2][:], bank[5 + tc][:],
                                            AF.Copy, bias=0.0, scale=f32c(QK_SCALE))
                        ins.then_inc(s_ac.h, 1)
            block.scalar(f_act)
            s_ac.n += 4 * NKV + 2 * NQ

            def f_qstore(sy):
                for h in range(NQ):
                    for tc in range(2):
                        sy.wait_ge(s_ac.h, ac_q(h, tc))
                        sy.dma_start(qbuf[h, :, tc * 512:(tc + 1) * 512],
                                     qs_b[(2 * h + tc) % 2][:]).then_inc(s_st.h, 16)
            block.sync(f_qstore)
            s_st.n += 16 * 2 * NQ

        def attn_phase():
            pe0, ac0, dv0, hl0, st0 = s_pe.n, s_ac.n, s_dv.n, s_hl.n, s_st.n
            PEQ, ACQ, DVQ = 9, 9, 4

            def pe_s(gq, h4):
                return pe0 + PEQ * gq + h4 + 1

            def pe_pt(gh):
                return pe0 + PEQ * (gh // 4) + 4 + (gh % 4) + 1

            def pe_av(gq):
                return pe0 + PEQ * gq + 9

            def ac_exp(gh):
                return ac0 + ACQ * (gh // 4) + (gh % 4) + 1

            def ac_pt(gh):
                return ac0 + ACQ * (gh // 4) + 4 + (gh % 4) + 1

            def ac_oev(gq):
                return ac0 + ACQ * gq + 9

            def dv_mask(gh):
                return dv0 + DVQ * gh + 1

            def dv_max(gh):
                return dv0 + DVQ * gh + 2

            def dv_pn(gh):
                return dv0 + DVQ * gh + 4

            def f_qload(sy):
                sy.dma_start(mask_sb[:], mask_in.rearrange("q p k -> p q k")
                             ).then_inc(s_hl.h, 16)
                for kh in range(NKV):
                    if kh >= 2:
                        sy.wait_ge(s_pe.h, pe_s(8 * (kh - 2) + 7, 3))
                    sy.dma_start(qg_b[kh % 2][:],
                                 qbuf[4 * kh:4 * kh + 4].rearrange("h p q -> p h q")
                                 ).then_inc(s_hl.h, 16)
            block.sync(f_qload)
            s_hl.n += 16 * (NKV + 1)

            def f_pe(t):
                for kh in range(NKV):
                    t.wait_ge(s_hl.h, hl0 + 16 * (kh + 2))
                    for qt in range(8):
                        gq = 8 * kh + qt
                        for h4 in range(4):
                            gh = 4 * gq + h4
                            par = h4 % 2
                            if gh >= 2:
                                t.wait_ge(s_dv.h, dv_mask(gh - 2))
                            ins = t.matmul(bank[2 * par][:],
                                           qg_b[kh % 2][:, h4, qt * 128:(qt + 1) * 128],
                                           k_sb[:, kh, qt * 128:qt * 128 + 512],
                                           start=True, stop=True)
                            ins2 = t.matmul(bank[2 * par + 1][:, 0:128],
                                            qg_b[kh % 2][:, h4, qt * 128:(qt + 1) * 128],
                                            k_sb[:, kh, qt * 128 + 512:qt * 128 + 640],
                                            start=True, stop=True)
                            ins2.then_inc(s_pe.h, 1)
                        for h4 in range(4):
                            gh = 4 * gq + h4
                            t.wait_ge(s_dv.h, dv_pn(gh))
                            if gh >= 2:
                                t.wait_ge(s_ac.h, ac_pt(gh - 2))
                            dstb = bank[4 + h4 % 2][:].bitcast(BF16)
                            for j in range(5):
                                ins = t.transpose(dstb[:, j * 128:(j + 1) * 128],
                                                  pn_sb[h4][:, j * 128:(j + 1) * 128],
                                                  idb[:])
                            ins.then_inc(s_pe.h, 1)
                        t.wait_ge(s_ac.h, ac_pt(4 * gq + 3))
                        ob = 6 + qt % 2
                        if gq >= 2:
                            t.wait_ge(s_ac.h, ac_oev(gq - 2))
                        for j in range(5):
                            ins = t.matmul(bank[ob][:],
                                           v_tok[:, kh, qt + j, :],
                                           pt_sb[:, j, :, :].rearrange(
                                               "p a c -> p (a c)"),
                                           start=(j == 0), stop=(j == 4))
                        ins.then_inc(s_pe.h, 1)
            block.tensor(f_pe)
            s_pe.n += PEQ * 8 * NKV


            def f_dve(v):
                for gq in range(8 * NKV):
                    kh, qt = divmod(gq, 8)
                    for h4 in range(4):
                        gh = 4 * gq + h4
                        par = h4
                        sb2 = h4 % 2
                        v.wait_ge(s_pe.h, pe_s(gq, h4))
                        if gh >= 4:
                            v.wait_ge(s_ac.h, ac_exp(gh - 4))
                        v.tensor_tensor(s_sb[par][:, 0:512], bank[2 * sb2][:],
                                        mask_sb[:, qt, 0:512], OP.add)
                        ins = v.tensor_tensor(s_sb[par][:, 512:640],
                                              bank[2 * sb2 + 1][:, 0:128],
                                              mask_sb[:, qt, 512:640], OP.add)
                        ins.then_inc(s_dv.h, 1)
                        v.wait_ge(s_dv.h, dv_mask(gh))
                        ins = v.reduce_max(MNEG[par], s_sb[par][:], axis=AX.X,
                                           negate=True)
                        ins.then_inc(s_dv.h, 1)
                        v.wait_ge(s_ac.h, ac_exp(gh))
                        if gh >= 4:
                            v.wait_ge(s_dv.h, dv_pn(gh - 4))
                        ins = v.reciprocal(RINV[par], SSUM[par])
                        ins.then_inc(s_dv.h, 1)
                        v.wait_ge(s_dv.h, dv0 + DVQ * gh + 3)
                        if gh >= 4:
                            v.wait_ge(s_pe.h, pe_pt(gh - 4))
                        ins = v.tensor_scalar(pn_sb[par][:], p_sb[par][:],
                                              RINV[par], None, OP.mult)
                        ins.then_inc(s_dv.h, 1)
            block.vector(f_dve)
            s_dv.n += DVQ * 4 * 8 * NKV

            def f_act(sc):
                for gq in range(8 * NKV):
                    kh, qt = divmod(gq, 8)
                    for h4 in range(4):
                        gh = 4 * gq + h4
                        par = h4
                        sc.wait_ge(s_dv.h, dv_max(gh))
                        if gh >= 4:
                            sc.wait_ge(s_dv.h, dv_pn(gh - 4))
                        ins = sc.activation(p_sb[par][:], s_sb[par][:], AF.Exp,
                                            bias=MNEG[par], scale=1.0,
                                            accum_out=SSUM[par])
                        ins.then_inc(s_ac.h, 1)
                    for h4 in range(4):
                        gh = 4 * gq + h4
                        sc.wait_ge(s_pe.h, pe_pt(gh))
                        if gq >= 1:
                            sc.wait_ge(s_pe.h, pe_av(gq - 1))
                        ins = sc.copy(pt_sb[:, :, h4, :],
                                      bank[4 + h4 % 2][:].bitcast(BF16)[:, 0:640]
                                      .rearrange("p (a c) -> p a c", a=5))
                        ins.then_inc(s_ac.h, 1)
                    sc.wait_ge(s_pe.h, pe_av(gq))
                    ins = sc.copy(obuf_res[:, 4 * kh:4 * kh + 4, qt * 128:(qt + 1) * 128],
                                  bank[6 + qt % 2][:].rearrange("p (a c) -> p a c",
                                                                a=4))
                    ins.then_inc(s_ac.h, 1)
            block.scalar(f_act)
            s_ac.n += ACQ * 8 * NKV



        def attn_debug_store():
            if obuf is None:
                return
            snap_ac = s_ac.n

            def f_dbg(sy):
                sy.wait_ge(s_ac.h, snap_ac)
                sy.dma_start(obuf.rearrange("h p q -> p h q"), obuf_res[:]
                             ).then_inc(s_st.h, 16)
            block.sync(f_dbg)
            s_st.n += 16

        def oproj_phase():
            wl0, pe0, ac0, dv0, hl0, st0 = (s_wl.n, s_pe.n, s_ac.n, s_dv.n,
                                            s_hl.n, s_st.n)
            n_tc = 2
            NT = 16 * n_tc
            order = []
            for dmc in range(16):
                order.append(("mm", dmc))
                if dmc >= 1:
                    order += [("tr", (dmc - 1) * n_tc + tc) for tc in range(n_tc)]
            order += [("tr", 15 * n_tc + tc) for tc in range(n_tc)]
            pe_at = {ev: pe0 + i + 1 for i, ev in enumerate(order)}

            def st3(g, g2):
                g.wait_ge(s_dv.h, dv0 + g2 + 1)
                g.dma_start(
                    yb[(g2 % n_tc) * 512:((g2 % n_tc) + 1) * 512,
                       (g2 // n_tc) * 256:((g2 // n_tc) + 1) * 256]
                    .rearrange("(a p) c -> p a c", p=128),
                    os3_b[g2 % 2][:]).then_inc(s_st.h, 16)

            def f_g(g):
                for dmc in range(16):
                    if dmc >= 2:
                        for g2 in range((dmc - 2) * n_tc, (dmc - 1) * n_tc):
                            st3(g, g2)
                    if dmc >= 2:
                        g.wait_ge(s_pe.h, pe_at[("mm", dmc - 2)])
                    g.dma_start(wo_b[dmc % 2][:], wo_t[dmc]).then_inc(s_wl.h, 16)
                for g2 in range(14 * n_tc, 16 * n_tc):
                    st3(g, g2)
            block.gpsimd(f_g)
            s_wl.n += 16 * 16
            s_st.n += 16 * NT

            def f_load(sy):
                for g2 in range(NT):
                    dmc, tc = divmod(g2, n_tc)
                    if g2 >= 6:
                        sy.wait_ge(s_dv.h, dv0 + g2 - 5)
                    sy.dma_start(
                        r3_b[g2 % 6][:],
                        ya[HALO + tc * 512:HALO + (tc + 1) * 512,
                           dmc * 256:(dmc + 1) * 256]
                        .rearrange("(a p) c -> p a c", p=128)).then_inc(s_hl.h, 16)
            block.sync(f_load)
            s_hl.n += 16 * NT

            def f_pe(t):
                for dmc in range(16):
                    t.wait_ge(s_wl.h, wl0 + 16 * (dmc + 1))
                    if dmc >= 1:
                        t.wait_ge(s_ac.h, ac0 + n_tc * dmc)
                    for fc in range(KC):
                        for dm2 in range(2):
                            for tc in range(n_tc):
                                ins = t.matmul(
                                    bank[dm2 * n_tc + tc][:],
                                    wo_b[dmc % 2][:, fc, dm2 * 128:(dm2 + 1) * 128],
                                    obuf_res[:, fc, tc * 512:(tc + 1) * 512],
                                    start=(fc == 0), stop=(fc == KC - 1))
                    ins.then_inc(s_pe.h, 1)
                    if dmc >= 1:
                        for tc in range(n_tc):
                            g2 = (dmc - 1) * n_tc + tc
                            t.wait_ge(s_ac.h, ac0 + g2 + 1)
                            if g2 >= 1:
                                t.wait_ge(s_dv.h, dv0 + g2)
                            for tt in range(4):
                                for dm2 in range(2):
                                    ins = t.transpose(
                                        bank[6 + dm2][:, tt * 128:(tt + 1) * 128],
                                        y3_b[g2 % 4][:, dm2, tt * 128:(tt + 1) * 128],
                                        idf[:])
                            ins.then_inc(s_pe.h, 1)
                for tc in range(n_tc):
                    g2 = 15 * n_tc + tc
                    t.wait_ge(s_ac.h, ac0 + g2 + 1)
                    t.wait_ge(s_dv.h, dv0 + g2)
                    for tt in range(4):
                        for dm2 in range(2):
                            ins = t.transpose(
                                bank[6 + dm2][:, tt * 128:(tt + 1) * 128],
                                y3_b[g2 % 4][:, dm2, tt * 128:(tt + 1) * 128],
                                idf[:])
                    ins.then_inc(s_pe.h, 1)
            block.tensor(f_pe)
            s_pe.n += len(order)

            def f_act(sc):
                for dmc in range(16):
                    sc.wait_ge(s_pe.h, pe_at[("mm", dmc)])
                    for tc in range(n_tc):
                        g2 = dmc * n_tc + tc
                        if g2 >= 4:
                            sc.wait_ge(s_pe.h, pe_at[("tr", g2 - 4)])
                        for dm2 in range(2):
                            ins = sc.copy(y3_b[g2 % 4][:, dm2, :],
                                          bank[dm2 * n_tc + tc][:])
                        ins.then_inc(s_ac.h, 1)
            block.scalar(f_act)
            s_ac.n += NT

            def f_dve(v):
                for g2 in range(NT):
                    v.wait_ge(s_pe.h, pe_at[("tr", g2)])
                    v.wait_ge(s_hl.h, hl0 + 16 * (g2 + 1))
                    if g2 >= 2:
                        v.wait_ge(s_st.h, st0 + 16 * (g2 - 1))
                    for tt in range(4):
                        for dm2 in range(2):
                            ins = v.tensor_tensor(
                                os3_b[g2 % 2][:, tt, dm2 * 128:(dm2 + 1) * 128],
                                bank[6 + dm2][:, tt * 128:(tt + 1) * 128],
                                r3_b[g2 % 6][:, tt, dm2 * 128:(dm2 + 1) * 128],
                                OP.add)
                    ins.then_inc(s_dv.h, 1)
            block.vector(f_dve)
            s_dv.n += NT

        if "A" in phases:
            with nc.named_scope("A0"):
                ln_phase(x_in, 8, xn_c)
                join_all()
            with nc.named_scope("A1"):
                ffn_up(xn_c, 2)
                join_all()
            with nc.named_scope("A2"):
                ffn_down(x_in, ya[HALO:HALO + T_OWN, :], 2)
                join_all()
            with nc.named_scope("AX"):
                def f_exch(g):
                    g.dma_start(cc_in, ya[T_OWN:T_OWN + HALO, :]).then_inc(s_hl.h, 16)
                    g.wait_ge(s_hl.h, s_hl.n + 16)
                    g.collective_compute(
                        "AllGather", OP.bypass,
                        replica_groups=[[0, 1, 2, 3, 4, 5, 6, 7]],
                        ins=[cc_in.opt()],
                        outs=[cc_out.opt()],
                    ).then_inc(s_wl.h)
                block.gpsimd(f_exch)
                s_hl.n += 16
                s_wl.n += 1

                def f_pick(sy):
                    sy.wait_ge(s_wl.h, s_wl.n)
                    pid = sy.partition_id()
                    idx = (pid + 7) % 8
                    cc3 = cc_out.rearrange("(a b) c -> a b c", b=HALO)
                    sy.dma_start(ya[0:HALO, :], cc3[idx]).then_inc(s_hl.h, 16)
                block.sync(f_pick)
                s_hl.n += 16
                join_all()

        bstg = set()
        for ph in phases:
            if ph == "B":
                bstg |= {"B0", "B1", "B2", "B3"}
            else:
                bstg.add(ph)
        if "B0" in bstg:
            with nc.named_scope("B0"):
                ln_phase(ya, 12, xn_ab)
                join_all()
        if "B1" in bstg:
            with nc.named_scope("B1"):
                proj_phase(xn_ab)
                join_all()
        if "B2" in bstg:
            with nc.named_scope("B2"):
                attn_phase()
                attn_debug_store()
                join_all()
        if "B3" in bstg:
            with nc.named_scope("B3"):
                oproj_phase()
                join_all()

        if "C" in phases:
            src_c = yb if "B3" in bstg else ya[0:T_OWN, :]
            with nc.named_scope("C0"):
                ln_phase(src_c, 8, xn_c)
                join_all()
            with nc.named_scope("C1"):
                ffn_up(xn_c, 2)
                join_all()
            with nc.named_scope("C2"):
                ffn_down(src_c, out, 2)
                join_all()

    return nc



def _prep_weights(wq, wk, wv, wo, wgate, wup, wdown):
    def blob(wT, n_out, chunk):
        return np.ascontiguousarray(
            wT.reshape(KC, 128, n_out // chunk, chunk).transpose(2, 1, 0, 3))

    wg_t = blob(np.ascontiguousarray(wgate.T), INTER, 256)
    wu_t = blob(np.ascontiguousarray(wup.T), INTER, 256)
    wdT = np.ascontiguousarray(wdown.T)
    wd_t = np.ascontiguousarray(wdT.reshape(86, 128, 8, 512).transpose(2, 1, 0, 3))
    wk_t = blob(np.ascontiguousarray(wk.T), NKV * HD, 128)
    wv_t = blob(np.ascontiguousarray(wv.T), NKV * HD, 128)
    wq_t = blob(np.ascontiguousarray(wq.T), NQ * HD, 128)
    wo_t = blob(np.ascontiguousarray(wo.T), D, 256)
    return dict(wg_t=wg_t, wu_t=wu_t, wd_t=wd_t, wk_t=wk_t, wv_t=wv_t,
                wq_t=wq_t, wo_t=wo_t)


def _masks(core):
    m = np.zeros((8, 128, 640), np.float32)
    j = np.arange(640)[None, :]
    qr = np.arange(128)[:, None]
    band = (j > qr) & (j <= qr + WIN)
    for qt in range(8):
        valid = band
        if core == 0:
            valid = valid & ((qt * 128 + j) >= HALO)
        m[qt] = np.where(valid, 0.0, NEG)
    return m


def kernel(L0_attn_x, L0_attn_wq, L0_attn_wk, L0_attn_wv, L0_attn_wo,
           L0_ffn_w_gate, L0_ffn_w_up, L0_ffn_w_down):
    x = np.asarray(L0_attn_x, np.float32).reshape(8192, D)
    wb = _prep_weights(np.asarray(L0_attn_wq, np.float32),
                       np.asarray(L0_attn_wk, np.float32),
                       np.asarray(L0_attn_wv, np.float32),
                       np.asarray(L0_attn_wo, np.float32),
                       np.asarray(L0_ffn_w_gate, np.float32),
                       np.asarray(L0_ffn_w_up, np.float32),
                       np.asarray(L0_ffn_w_down, np.float32))
    ident = np.eye(128, dtype=np.float32)

    nc = build_nc()
    in_maps = []
    for c in range(8):
        m = dict(wb)
        m["x_in"] = np.ascontiguousarray(x[c * T_OWN: (c + 1) * T_OWN])
        m["mask_in"] = _masks(c)
        m["ident_in"] = ident
        in_maps.append(m)
    res = run_bass_kernel_spmd(nc, in_maps, core_ids=list(range(8)))
    outp = np.concatenate([res.results[c]["out"] for c in range(8)], axis=0)
    return outp.reshape(1, 8192, D)


# version 06
# speedup vs baseline: 1.0405x; 1.0405x over previous
import os
import numpy as np
import ml_dtypes
import concourse.bass as bass
import concourse.mybir as mybir
from concourse.bass_utils import run_bass_kernel_spmd

F32 = mybir.dt.float32
BF16 = mybir.dt.bfloat16
U8 = mybir.dt.uint8
AF = mybir.ActivationFunctionType
AX = mybir.AxisListType
OP = mybir.AluOpType

D = 4096
KC = 32
INTER = 11008
NIC = 43
T_A = 1536
T_OWN = 1024
HALO = 512
NQ, NKV, HD, WIN = 32, 8, 128, 512
QK_SCALE = float(1.0 / np.sqrt(128.0))
NEG = -1.0e30
KB = 1024
f32c = np.float32
SEM_NAMES = {}


def _carve(arena, off, shape, dtype):
    nb = int(np.prod(shape[1:])) * mybir.dt.size(dtype)
    assert off % 32 == 0
    v = arena[:, off:off + nb].bitcast(dtype)
    if len(shape) > 2:
        dims = " ".join(f"d{i}" for i in range(1, len(shape)))
        v = v.rearrange(f"p ({dims}) -> p {dims}",
                        **{f"d{i}": shape[i] for i in range(1, len(shape))})
    return v, off + ((nb + 31) // 32) * 32


class Ctr:
    def __init__(self, h):
        self.h = h
        self.n = 0

    def inc(self, ins, k=1):
        ins.then_inc(self.h, k)
        self.n += k
        return self.n


def build_nc(phases=("A", "B", "C"), expose_ya=False, expose_yb=False,
             expose_q=False, expose_o=False):
    nc = bass.Bass("TRN2", target_bir_lowering=False)

    x_in = nc.dram_tensor("x_in", [T_OWN, D], F32, kind="ExternalInput").ap()
    wg_t = nc.dram_tensor("wg_t", [NIC, 128, KC, 256], F32, kind="ExternalInput").ap()
    wu_t = nc.dram_tensor("wu_t", [NIC, 128, KC, 256], F32, kind="ExternalInput").ap()
    wd_t = nc.dram_tensor("wd_t", [8, 128, 86, 512], F32, kind="ExternalInput").ap()
    wk_t = nc.dram_tensor("wk_t", [NKV, 128, KC, 128], F32, kind="ExternalInput").ap()
    wv_t = nc.dram_tensor("wv_t", [NKV, 128, KC, 128], F32, kind="ExternalInput").ap()
    wq_t = nc.dram_tensor("wq_t", [NQ, 128, KC, 128], F32, kind="ExternalInput").ap()
    wo_t = nc.dram_tensor("wo_t", [16, 128, KC, 256], F32, kind="ExternalInput").ap()
    mask_in = nc.dram_tensor("mask_in", [8, 128, 640], F32, kind="ExternalInput").ap()
    ident_in = nc.dram_tensor("ident_in", [128, 128], F32, kind="ExternalInput").ap()

    out = nc.dram_tensor("out", [T_OWN, D], F32, kind="ExternalOutput").ap()
    ya = nc.dram_tensor("ya", [T_A, D], F32,
                        kind="ExternalOutput" if expose_ya else "Internal").ap()
    yb = nc.dram_tensor("yb", [T_OWN, D], F32,
                        kind="ExternalOutput" if expose_yb else "Internal").ap()
    hbuf = nc.dram_tensor("hbuf", [86, 128, T_A], BF16, kind="Internal").ap()
    cc_in = nc.dram_tensor("cc_in", [HALO, D], F32, kind="Internal").ap()
    cc_out = nc.dram_tensor("cc_out", [8 * HALO, D], F32, kind="Internal",
                        addr_space="Shared").ap()
    qbuf = nc.dram_tensor("qbuf", [NQ, 128, T_OWN], BF16,
                          kind="ExternalOutput" if expose_q else "Internal").ap()
    obuf = (nc.dram_tensor("obuf", [NQ, 128, T_OWN], BF16,
                           kind="ExternalOutput").ap() if expose_o else None)

    from contextlib import ExitStack
    ctx = ExitStack()
    with ctx:
        arena = ctx.enter_context(nc.sbuf_tensor("arena", [128, 192 * KB], U8)).ap()
        bank = [ctx.enter_context(nc.psum_tensor(f"bank{i}", [128, 512], F32)).ap()
                for i in range(8)]

        o = 0
        idb, o = _carve(arena, o, [128, 128], BF16)
        idf, o = _carve(arena, o, [128, 128], F32)
        stats, o = _carve(arena, o, [128, 32], F32)
        assert o <= 1 * KB
        RAWS = stats[:, 0:1]
        MEAN = stats[:, 1:2]
        SUMSQ = stats[:, 2:3]
        VAR = stats[:, 3:4]
        M2 = stats[:, 4:5]
        RSTD_S = stats[:, 5:6]
        RSTD = stats[:, 6:7]
        MNEG = [stats[:, 8 + 4 * i:9 + 4 * i] for i in range(4)]
        SSUM = [stats[:, 9 + 4 * i:10 + 4 * i] for i in range(4)]
        RINV = [stats[:, 10 + 4 * i:11 + 4 * i] for i in range(4)]

        xn_ab, _ = _carve(arena, 1 * KB, [128, KC, T_A], BF16)
        xn_c, _ = _carve(arena, 1 * KB, [128, KC, T_OWN], BF16)
        wd_cache, _ = _carve(arena, 1 * KB, [128, 86, 512], BF16)
        o2 = 1 * KB
        qg_b = []
        for i in range(2):
            v, o2 = _carve(arena, o2, [128, 4, T_OWN], BF16)
            qg_b.append(v)
        obuf_res, o2 = _carve(arena, o2, [128, NQ, T_OWN], BF16)
        s_sb, p_sb, pn_sb = [], [], []
        for i in range(4):
            v, o2 = _carve(arena, o2, [128, 640], F32)
            s_sb.append(v)
        assert o2 <= 97 * KB, o2
        o2w = 145 * KB
        for i in range(4):
            v, o2w = _carve(arena, o2w, [128, 640], F32)
            p_sb.append(v)
        pt_sb, o2w = _carve(arena, o2w, [128, 5, 4, 128], BF16)
        assert o2w <= 161 * KB, o2w
        o2s = 161 * KB
        mask_sb, o2s = _carve(arena, o2s, [128, 8, 640], F32)
        for i in range(8):
            v, o2s = _carve(arena, o2s, [128, 640], BF16)
            pn_sb.append(v)
        assert o2s <= 192 * KB, o2s
        o3 = 97 * KB
        wo_b = []
        for i in range(2):
            v, o3 = _carve(arena, o3, [128, KC, 256], BF16)
            wo_b.append(v)
        y3_b = []
        for i in range(4):
            v, o3 = _carve(arena, o3, [128, 2, 512], F32)
            y3_b.append(v)
        r3_b = []
        for i in range(6):
            v, o3 = _carve(arena, o3, [128, 4, 256], F32)
            r3_b.append(v)
        os3_b = []
        for i in range(2):
            v, o3 = _carve(arena, o3, [128, 4, 256], F32)
            os3_b.append(v)
        assert o3 <= 184 * KB, o3

        W1 = 97 * KB
        o4 = W1
        wgb, wub = [], []
        for i in range(2):
            v, o4 = _carve(arena, o4, [128, KC, 256], BF16)
            wgb.append(v)
        for i in range(2):
            v, o4 = _carve(arena, o4, [128, KC, 256], BF16)
            wub.append(v)
        assert o4 <= 161 * KB
        o5 = W1
        trash, o5 = _carve(arena, o5, [128, D], F32)
        xt_b, xnb_b = [], []
        for i in range(2):
            v, o5 = _carve(arena, o5, [128, D], F32)
            xt_b.append(v)
        for i in range(2):
            v, o5 = _carve(arena, o5, [128, D], BF16)
            xnb_b.append(v)
        assert o5 <= 161 * KB
        o6 = W1
        hch_b = []
        for i in range(2):
            row = []
            for tc in range(2):
                v, o6 = _carve(arena, o6, [128, 8, 512], BF16)
                row.append(v)
            hch_b.append(row)
        y2_b = []
        for i in range(4):
            v, o6 = _carve(arena, o6, [128, 4, 512], F32)
            y2_b.append(v)
        assert o6 <= 161 * KB
        o6x = 1 * KB
        wdch_b = []
        for i in range(2):
            v, o6x = _carve(arena, o6x, [128, 8, 512], BF16)
            wdch_b.append(v)
        r2_b, os2_b = [], []
        for i in range(4):
            v, o6x = _carve(arena, o6x, [128, 4, 512], F32)
            r2_b.append(v)
        for i in range(2):
            v, o6x = _carve(arena, o6x, [128, 4, 512], F32)
            os2_b.append(v)
        assert o6x <= 97 * KB
        o7 = W1
        k_sb, o7 = _carve(arena, o7, [128, NKV, T_A], BF16)
        v_tok, o7 = _carve(arena, o7, [128, NKV, 12, 128], BF16)
        wpj_b = []
        for i in range(2):
            v, o7 = _carve(arena, o7, [128, KC, 128], BF16)
            wpj_b.append(v)
        assert o7 <= 161 * KB
        o9 = 161 * KB
        vf_sb, o9 = _carve(arena, o9, [128, T_A], BF16)
        qs_b = []
        for i in range(2):
            v, o9 = _carve(arena, o9, [128, 512], BF16)
            qs_b.append(v)
        gs_sb, o9 = _carve(arena, o9, [128, 3, 512], F32)
        hs_b = []
        for i in range(2):
            v, o9 = _carve(arena, o9, [128, 3, 512], BF16)
            hs_b.append(v)
        assert o9 <= 180 * KB, o9

        block = ctx.enter_context(nc.Block())

        s_wl = Ctr(ctx.enter_context(nc.semaphore("s_wl")))
        s_hl = Ctr(ctx.enter_context(nc.semaphore("s_hl")))
        s_st = Ctr(ctx.enter_context(nc.semaphore("s_st")))
        s_pe = Ctr(ctx.enter_context(nc.semaphore("s_pe")))
        s_ac = Ctr(ctx.enter_context(nc.semaphore("s_ac")))
        s_dv = Ctr(ctx.enter_context(nc.semaphore("s_dv")))
        ALL = [s_wl, s_hl, s_st, s_pe, s_ac, s_dv]
        global SEM_NAMES
        SEM_NAMES = {c.h.num: n for c, n in zip(ALL, ["wl", "hl", "st", "pe", "ac", "dv"])}

        def join_all():
            snap = [(c.h, c.n) for c in ALL]

            def wait_all(e):
                for h, n in snap:
                    if n:
                        e.wait_ge(h, n)
            block.sync(wait_all)
            block.gpsimd(wait_all)
            block.tensor(wait_all)
            block.scalar(wait_all)
            block.vector(wait_all)

        def p_pre_g(g):
            g.dma_start(idb[:], ident_in).then_inc(s_wl.h, 16)
        block.gpsimd(p_pre_g)
        s_wl.n += 16

        def p_pre_s(sy):
            sy.dma_start(idf[:], ident_in).then_inc(s_hl.h, 16)
        block.sync(p_pre_s)
        s_hl.n += 16
        join_all()

        def ln_phase(src, n_tiles, xn):
            hl0, ac0, dv0, pe0 = s_hl.n, s_ac.n, s_dv.n, s_pe.n
            DVP, ACP, PEP = 6, 6, 4

            def f_sync(sy):
                for tt in range(n_tiles):
                    if tt >= 2:
                        sy.wait_ge(s_dv.h, dv0 + DVP * (tt - 2) + 6)
                        sy.wait_ge(s_ac.h, ac0 + ACP * (tt - 2) + 1)
                    sy.dma_start(xt_b[tt % 2][:], src[tt * 128:(tt + 1) * 128, :]
                                 ).then_inc(s_hl.h, 16)
            block.sync(f_sync)
            s_hl.n += 16 * n_tiles

            def f_dve(v):
                for tt in range(n_tiles):
                    x_t = xt_b[tt % 2]
                    v.wait_ge(s_hl.h, hl0 + 16 * (tt + 1))
                    if tt >= 1:
                        v.wait_ge(s_dv.h, dv0 + DVP * tt)
                    ins = v.reduce_sum(RAWS, x_t[:], axis=AX.X)
                    ins.then_inc(s_dv.h, 1)
                    v.wait_ge(s_dv.h, dv0 + DVP * tt + 1)
                    ins = v.tensor_scalar_mul(MEAN, RAWS, 1.0 / D)
                    ins.then_inc(s_dv.h, 1)
                    v.wait_ge(s_ac.h, ac0 + ACP * tt + 1)
                    v.tensor_scalar(VAR, SUMSQ, 1.0 / D, None, OP.mult)
                    v.wait_ge(s_dv.h, dv0 + DVP * tt + 2)
                    ins = v.tensor_mul(M2, MEAN, MEAN)
                    ins.then_inc(s_dv.h, 1)
                    v.wait_ge(s_dv.h, dv0 + DVP * tt + 3)
                    ins = v.tensor_tensor(VAR, VAR, M2, OP.subtract)
                    ins.then_inc(s_dv.h, 1)
                    v.wait_ge(s_ac.h, ac0 + ACP * tt + 2)
                    ins = v.reciprocal(RSTD, RSTD_S)
                    ins.then_inc(s_dv.h, 1)
                    if tt >= 2:
                        v.wait_ge(s_pe.h, pe0 + PEP * (tt - 2) + 4)
                    v.wait_ge(s_dv.h, dv0 + DVP * tt + 5)
                    ins = v.tensor_scalar(xnb_b[tt % 2][:], x_t[:], MEAN, RSTD,
                                          OP.subtract, OP.mult)
                    ins.then_inc(s_dv.h, 1)
            block.vector(f_dve)
            s_dv.n += DVP * n_tiles

            def f_act(sc):
                for tt in range(n_tiles):
                    x_t = xt_b[tt % 2]
                    sc.wait_ge(s_hl.h, hl0 + 16 * (tt + 1))
                    if tt >= 1:
                        sc.wait_ge(s_dv.h, dv0 + DVP * (tt - 1) + 4)
                    ins = sc.activation(trash[:], x_t[:], AF.Square,
                                        bias=0.0, scale=1.0, accum_out=SUMSQ)
                    ins.then_inc(s_ac.h, 1)
                    sc.wait_ge(s_dv.h, dv0 + DVP * tt + 4)
                    ins = sc.activation(RSTD_S, VAR, AF.Sqrt, bias=f32c(1e-5),
                                        scale=1.0)
                    ins.then_inc(s_ac.h, 1)
                    for b in range(4):
                        sc.wait_ge(s_pe.h, pe0 + PEP * tt + b + 1)
                        src_ps = bank[6 + b % 2][:].bitcast(BF16)[:, 0:1024].rearrange(
                            "p (a c) -> p a c", a=8)
                        ins = sc.copy(xn[:, b * 8:(b + 1) * 8, tt * 128:(tt + 1) * 128],
                                      src_ps)
                        ins.then_inc(s_ac.h, 1)
            block.scalar(f_act)
            s_ac.n += ACP * n_tiles

            def f_pe(t):
                for tt in range(n_tiles):
                    t.wait_ge(s_dv.h, dv0 + DVP * tt + 6)
                    for b in range(4):
                        gb = tt * 4 + b
                        if gb >= 2:
                            tp, bp = divmod(gb - 2, 4)
                            t.wait_ge(s_ac.h, ac0 + ACP * tp + 3 + bp)
                        dst = bank[6 + b % 2][:].bitcast(BF16)
                        for j in range(8):
                            kc = b * 8 + j
                            ins = t.transpose(dst[:, j * 128:(j + 1) * 128],
                                              xnb_b[tt % 2][:, kc * 128:(kc + 1) * 128],
                                              idb[:])
                        ins.then_inc(s_pe.h, 1)
            block.tensor(f_pe)
            s_pe.n += PEP * n_tiles

        def ffn_up(xn, n_tc):
            wl0, pe0, ac0, dv0, st0 = s_wl.n, s_pe.n, s_ac.n, s_dv.n, s_st.n
            NG = 2 * NIC

            def f_g(g):
                for ic in range(NIC):
                    if ic >= 2:
                        g.wait_ge(s_pe.h, pe0 + 4 * ic - 4)
                    g.dma_start(wgb[ic % 2][:], wg_t[ic]).then_inc(s_wl.h, 16)
                    g.dma_start(wub[ic % 2][:], wu_t[ic]).then_inc(s_wl.h, 16)
            block.gpsimd(f_g)
            s_wl.n += 32 * NIC

            def f_pe(t):
                for ic in range(NIC):
                    t.wait_ge(s_wl.h, wl0 + 32 * (ic + 1))
                    for i2 in range(2):
                        gi = 2 * ic + i2
                        if gi >= 1:
                            t.wait_ge(s_ac.h, ac0 + gi)
                        for kc in range(KC):
                            for tc in range(n_tc):
                                ins = t.matmul(bank[tc][:],
                                               wgb[ic % 2][:, kc, i2 * 128:(i2 + 1) * 128],
                                               xn[:, kc, tc * 512:(tc + 1) * 512],
                                               start=(kc == 0), stop=(kc == KC - 1))
                        ins.then_inc(s_pe.h, 1)
                        if gi >= 1:
                            t.wait_ge(s_dv.h, dv0 + gi)
                        for kc in range(KC):
                            for tc in range(n_tc):
                                ins = t.matmul(bank[3 + tc][:],
                                               wub[ic % 2][:, kc, i2 * 128:(i2 + 1) * 128],
                                               xn[:, kc, tc * 512:(tc + 1) * 512],
                                               start=(kc == 0), stop=(kc == KC - 1))
                        ins.then_inc(s_pe.h, 1)
            block.tensor(f_pe)
            s_pe.n += 2 * NG

            def f_act(sc):
                for gi in range(NG):
                    sc.wait_ge(s_pe.h, pe0 + 2 * gi + 1)
                    if gi >= 1:
                        sc.wait_ge(s_dv.h, dv0 + gi)
                    for tc in range(n_tc):
                        ins = sc.activation(gs_sb[:, tc, :], bank[tc][:], AF.Silu,
                                            bias=0.0, scale=1.0)
                    ins.then_inc(s_ac.h, 1)
            block.scalar(f_act)
            s_ac.n += NG

            def f_dve(v):
                for gi in range(NG):
                    v.wait_ge(s_pe.h, pe0 + 2 * gi + 2)
                    v.wait_ge(s_ac.h, ac0 + gi + 1)
                    if gi >= 2:
                        v.wait_ge(s_st.h, st0 + 16 * (gi - 1))
                    for tc in range(n_tc):
                        ins = v.scalar_tensor_tensor(hs_b[gi % 2][:, tc, :],
                                                     bank[3 + tc][:], 0.5,
                                                     gs_sb[:, tc, :],
                                                     OP.mult, OP.mult)
                    ins.then_inc(s_dv.h, 1)
            block.vector(f_dve)
            s_dv.n += NG

            def f_store(sy):
                for gi in range(NG):
                    sy.wait_ge(s_dv.h, dv0 + gi + 1)
                    sy.dma_start(hbuf[gi, :, 0:n_tc * 512],
                                 hs_b[gi % 2][:, 0:n_tc, :].rearrange("p a c -> p (a c)")
                                 ).then_inc(s_st.h, 16)
            block.sync(f_store)
            s_st.n += 16 * NG

        def ffn_down(res_src, dst, n_tc):
            assert n_tc == 2
            wl0, pe0, ac0, dv0, hl0, st0 = (s_wl.n, s_pe.n, s_ac.n, s_dv.n,
                                            s_hl.n, s_st.n)
            NB = 11
            NT = 16

            def pe_oct(dmc, j):
                return pe0 + 13 * dmc + j + 1

            def pe_mm(dmc):
                return pe_oct(dmc, NB - 1)

            def pe_tr(g2):
                return pe0 + 13 * (g2 // 2) + NB + (g2 % 2) + 1

            def f_g(g):
                for dmc in range(8):
                    if dmc >= 1:
                        for g2 in range(2 * (dmc - 1), 2 * dmc):
                            g.wait_ge(s_dv.h, dv0 + 16 * (g2 + 1))
                            g.dma_start(
                                dst[(g2 % 2) * 512:((g2 % 2) + 1) * 512,
                                    (g2 // 2) * 512:((g2 // 2) + 1) * 512]
                                .rearrange("(a p) c -> p a c", p=128),
                                os2_b[g2 % 2][:]).then_inc(s_st.h, 16)
                    for j in range(NB):
                        go = dmc * NB + j
                        if go >= 2:
                            g.wait_ge(s_pe.h, pe_oct((go - 2) // NB, (go - 2) % NB))
                        nic8 = 8 if j < 10 else 6
                        g.dma_start(wdch_b[go % 2][:, 0:nic8, :],
                                    wd_t[dmc][:, j * 8:j * 8 + nic8, :]
                                    ).then_inc(s_wl.h, 16)
                for g2 in range(14, 16):
                    g.wait_ge(s_dv.h, dv0 + 16 * (g2 + 1))
                    g.dma_start(
                        dst[(g2 % 2) * 512:((g2 % 2) + 1) * 512,
                            (g2 // 2) * 512:((g2 // 2) + 1) * 512]
                        .rearrange("(a p) c -> p a c", p=128),
                        os2_b[g2 % 2][:]).then_inc(s_st.h, 16)
            block.gpsimd(f_g)
            s_wl.n += 16 * 8 * NB
            s_st.n += 16 * NT

            def hl_at(dmc, j, tc):
                return hl0 + 16 * (24 * dmc + 2 + 2 * j + tc + 1)

            def f_load(sy):
                for dmc in range(8):
                    for tc in range(2):
                        g2 = 2 * dmc + tc
                        if g2 >= 4:
                            sy.wait_ge(s_dv.h, dv0 + 16 * (g2 - 3))
                        sy.dma_start(
                            r2_b[g2 % 4][:],
                            res_src[tc * 512:(tc + 1) * 512,
                                    dmc * 512:(dmc + 1) * 512]
                            .rearrange("(a p) c -> p a c", p=128)
                            ).then_inc(s_hl.h, 16)
                    for j in range(NB):
                        go = dmc * NB + j
                        if go >= 2:
                            sy.wait_ge(s_pe.h, pe_oct((go - 2) // NB, (go - 2) % NB))
                        nic8 = 8 if j < 10 else 6
                        for tc in range(2):
                            sy.dma_start(
                                hch_b[go % 2][tc][:, 0:nic8, :],
                                hbuf[j * 8:j * 8 + nic8, :, tc * 512:(tc + 1) * 512]
                                .rearrange("a p c -> p a c")).then_inc(s_hl.h, 16)
            block.sync(f_load)
            s_hl.n += 16 * 8 * 24

            def f_pe(t):
                for dmc in range(8):
                    if dmc >= 1:
                        t.wait_ge(s_dv.h, dv0 + 16 * 2 * dmc)
                    for j in range(NB):
                        go = dmc * NB + j
                        t.wait_ge(s_wl.h, wl0 + 16 * (go + 1))
                        t.wait_ge(s_hl.h, hl_at(dmc, j, 1))
                        nic8 = 8 if j < 10 else 6
                        for i8 in range(nic8):
                            ic = j * 8 + i8
                            for dm2 in range(4):
                                for tc in range(2):
                                    ins = t.matmul(
                                        bank[dm2 * 2 + tc][:],
                                        wdch_b[go % 2][:, i8, dm2 * 128:(dm2 + 1) * 128],
                                        hch_b[go % 2][tc][:, i8, :],
                                        start=(ic == 0), stop=(ic == 85))
                        ins.then_inc(s_pe.h, 1)
                    for tc in range(2):
                        g2 = 2 * dmc + tc
                        t.wait_ge(s_ac.h, ac0 + g2 + 1)
                        for tt in range(4):
                            for dm2 in range(4):
                                ins = t.transpose(
                                    bank[dm2 * 2 + tc][:, tt * 128:(tt + 1) * 128],
                                    y2_b[g2 % 4][:, dm2, tt * 128:(tt + 1) * 128],
                                    idf[:])
                        ins.then_inc(s_pe.h, 1)
            block.tensor(f_pe)
            s_pe.n += 13 * 8

            def f_act(sc):
                for dmc in range(8):
                    for tc in range(2):
                        g2 = 2 * dmc + tc
                        sc.wait_ge(s_pe.h, pe_mm(dmc))
                        if g2 >= 4:
                            sc.wait_ge(s_pe.h, pe_tr(g2 - 4))
                        for dm2 in range(4):
                            ins = sc.copy(y2_b[g2 % 4][:, dm2, :],
                                          bank[dm2 * 2 + tc][:])
                        ins.then_inc(s_ac.h, 1)
            block.scalar(f_act)
            s_ac.n += NT

            def f_dve(v):
                for g2 in range(NT):
                    dmc, tc = divmod(g2, 2)
                    v.wait_ge(s_pe.h, pe_tr(g2))
                    v.wait_ge(s_hl.h, hl0 + 16 * (24 * dmc + tc + 1))
                    if g2 >= 2:
                        v.wait_ge(s_st.h, st0 + 16 * (g2 - 1))
                    for tt in range(4):
                        for dm2 in range(4):
                            ins = v.tensor_tensor(
                                os2_b[g2 % 2][:, tt, dm2 * 128:(dm2 + 1) * 128],
                                bank[dm2 * 2 + tc][:, tt * 128:(tt + 1) * 128],
                                r2_b[g2 % 4][:, tt, dm2 * 128:(dm2 + 1) * 128],
                                OP.add)
                    ins.then_inc(s_dv.h, 16)
            block.vector(f_dve)
            s_dv.n += 16 * NT

        def proj_phase(xn):
            wl0, pe0, ac0, dv0, st0 = s_wl.n, s_pe.n, s_ac.n, s_dv.n, s_st.n

            def pe_k(kh):
                return pe0 + kh + 1

            def pe_vp(kh):
                return pe0 + NKV + 2 * kh + 1

            def pe_vt(kh):
                return pe0 + NKV + 2 * kh + 2

            def pe_q(h):
                return pe0 + 3 * NKV + h + 1

            def ac_k(kh):
                return ac0 + kh + 1

            def ac_vf(kh):
                return ac0 + NKV + 3 * kh + 1

            def ac_vt(kh, i):
                return ac0 + NKV + 3 * kh + 2 + i

            def ac_q(h, tc):
                return ac0 + 4 * NKV + 2 * h + tc + 1

            def f_w(g):
                for kh in range(NKV):
                    if kh >= 2:
                        g.wait_ge(s_pe.h, pe_k(kh - 2))
                    g.dma_start(wpj_b[kh % 2][:], wk_t[kh]).then_inc(s_wl.h, 16)
                for kh in range(NKV):
                    g.wait_ge(s_pe.h, pe_k(6 + kh % 2) if kh < 2
                              else pe_vt(kh - 2))
                    g.dma_start(wpj_b[kh % 2][:], wv_t[kh]).then_inc(s_wl.h, 16)
                for h in range(NQ):
                    g.wait_ge(s_pe.h, pe_vt(6 + h % 2) if h < 2
                              else pe_q(h - 2))
                    g.dma_start(wpj_b[h % 2][:], wq_t[h]).then_inc(s_wl.h, 16)
            block.gpsimd(f_w)
            s_wl.n += 16 * (2 * NKV + NQ)

            def f_pe(t):
                for kh in range(NKV):
                    t.wait_ge(s_wl.h, wl0 + 16 * (kh + 1))
                    if kh >= 1:
                        t.wait_ge(s_ac.h, ac_k(kh - 1))
                    for kc in range(KC):
                        for tc in range(3):
                            ins = t.matmul(bank[tc][:],
                                           wpj_b[kh % 2][:, kc, :],
                                           xn[:, kc, tc * 512:(tc + 1) * 512],
                                           start=(kc == 0), stop=(kc == KC - 1))
                    ins.then_inc(s_pe.h, 1)
                for kh in range(NKV):
                    t.wait_ge(s_wl.h, wl0 + 16 * NKV + 16 * (kh + 1))
                    t.wait_ge(s_ac.h, ac_vf(kh - 1) if kh >= 1 else ac_k(NKV - 1))
                    for kc in range(KC):
                        for tc in range(3):
                            ins = t.matmul(bank[tc][:],
                                           wpj_b[kh % 2][:, kc, :],
                                           xn[:, kc, tc * 512:(tc + 1) * 512],
                                           start=(kc == 0), stop=(kc == KC - 1))
                    ins.then_inc(s_pe.h, 1)
                    t.wait_ge(s_ac.h, ac_vf(kh))
                    if kh >= 1:
                        t.wait_ge(s_ac.h, ac_vt(kh - 1, 1))
                    for tt in range(12):
                        dstb = bank[3 + tt // 8][:].bitcast(BF16)
                        ins = t.transpose(dstb[:, (tt % 8) * 128:(tt % 8 + 1) * 128],
                                          vf_sb[:, tt * 128:(tt + 1) * 128], idb[:])
                    ins.then_inc(s_pe.h, 1)
                for h in range(NQ):
                    t.wait_ge(s_wl.h, wl0 + 32 * NKV + 16 * (h + 1))
                    if h >= 1:
                        t.wait_ge(s_ac.h, ac_q(h - 1, 1))
                    for kc in range(KC):
                        for tc in range(2):
                            ins = t.matmul(bank[5 + tc][:],
                                           wpj_b[h % 2][:, kc, :],
                                           xn[:, kc, HALO + tc * 512:HALO + (tc + 1) * 512],
                                           start=(kc == 0), stop=(kc == KC - 1))
                    ins.then_inc(s_pe.h, 1)
            block.tensor(f_pe)
            s_pe.n += 3 * NKV + NQ

            def f_act(sc):
                for kh in range(NKV):
                    sc.wait_ge(s_pe.h, pe_k(kh))
                    for tc in range(3):
                        ins = sc.copy(k_sb[:, kh, tc * 512:(tc + 1) * 512],
                                      bank[tc][:])
                    ins.then_inc(s_ac.h, 1)
                for kh in range(NKV):
                    sc.wait_ge(s_pe.h, pe_vp(kh))
                    if kh >= 1:
                        sc.wait_ge(s_pe.h, pe_vt(kh - 1))
                    for tc in range(3):
                        ins = sc.copy(vf_sb[:, tc * 512:(tc + 1) * 512], bank[tc][:])
                    ins.then_inc(s_ac.h, 1)
                    sc.wait_ge(s_pe.h, pe_vt(kh))
                    ins = sc.copy(v_tok[:, kh, 0:8, :],
                                  bank[3][:].bitcast(BF16)[:, 0:1024].rearrange(
                                      "p (a c) -> p a c", a=8))
                    ins.then_inc(s_ac.h, 1)
                    ins = sc.copy(v_tok[:, kh, 8:12, :],
                                  bank[4][:].bitcast(BF16)[:, 0:512].rearrange(
                                      "p (a c) -> p a c", a=4))
                    ins.then_inc(s_ac.h, 1)
                for h in range(NQ):
                    sc.wait_ge(s_pe.h, pe_q(h))
                    for tc in range(2):
                        qi = 2 * h + tc
                        if qi >= 2:
                            sc.wait_ge(s_st.h, st0 + 16 * (qi - 1))
                        ins = sc.activation(qs_b[qi % 2][:], bank[5 + tc][:],
                                            AF.Copy, bias=0.0, scale=f32c(QK_SCALE))
                        ins.then_inc(s_ac.h, 1)
            block.scalar(f_act)
            s_ac.n += 4 * NKV + 2 * NQ

            def f_qstore(sy):
                for h in range(NQ):
                    for tc in range(2):
                        sy.wait_ge(s_ac.h, ac_q(h, tc))
                        sy.dma_start(qbuf[h, :, tc * 512:(tc + 1) * 512],
                                     qs_b[(2 * h + tc) % 2][:]).then_inc(s_st.h, 16)
            block.sync(f_qstore)
            s_st.n += 16 * 2 * NQ

        def attn_phase():
            pe0, ac0, dv0, hl0, st0 = s_pe.n, s_ac.n, s_dv.n, s_hl.n, s_st.n
            PEQ, ACQ, DVQ = 9, 9, 4

            NGQ = 8 * NKV

            def pe_s(gq, h4):
                return pe0 + 4 * gq + 5 * max(0, gq - 1) + h4 + 1

            def pe_pt(gh):
                gq, h4 = divmod(gh, 4)
                ns = 4 * min(gq + 2, NGQ)
                return pe0 + ns + 5 * gq + h4 + 1

            def pe_av(gq):
                ns = 4 * min(gq + 2, NGQ)
                return pe0 + ns + 5 * gq + 5

            def ac_exp(gh):
                gq = gh // 4
                return ac0 + 4 * gq + 5 * max(0, gq - 1) + gh % 4 + 1

            def ac_pt(gh):
                gq, h4 = divmod(gh, 4)
                ns = 4 * min(gq + 2, 8 * NKV)
                return ac0 + ns + 5 * gq + h4 + 1

            def ac_oev(gq):
                ns = 4 * min(gq + 2, 8 * NKV)
                return ac0 + ns + 5 * gq + 5

            def dv_mask(gh):
                return dv0 + DVQ * gh + 1

            def dv_max(gh):
                return dv0 + DVQ * gh + 2

            def dv_pn(gh):
                return dv0 + DVQ * gh + 4

            def f_qload(sy):
                sy.dma_start(mask_sb[:], mask_in.rearrange("q p k -> p q k")
                             ).then_inc(s_hl.h, 16)
                for kh in range(NKV):
                    if kh >= 2:
                        sy.wait_ge(s_pe.h, pe_s(8 * (kh - 2) + 7, 3))
                    sy.dma_start(qg_b[kh % 2][:],
                                 qbuf[4 * kh:4 * kh + 4].rearrange("h p q -> p h q")
                                 ).then_inc(s_hl.h, 16)
            block.sync(f_qload)
            s_hl.n += 16 * (NKV + 1)

            def f_pe(t):
                def s_block(gq):
                    kh, qt = divmod(gq, 8)
                    if qt == 0:
                        t.wait_ge(s_hl.h, hl0 + 16 * (kh + 2))
                    for h4 in range(4):
                        gh = 4 * gq + h4
                        par = h4 % 2
                        if gh >= 2:
                            t.wait_ge(s_dv.h, dv_mask(gh - 2))
                        ins = t.matmul(bank[2 * par][:],
                                       qg_b[kh % 2][:, h4, qt * 128:(qt + 1) * 128],
                                       k_sb[:, kh, qt * 128:qt * 128 + 512],
                                       start=True, stop=True)
                        ins2 = t.matmul(bank[2 * par + 1][:, 0:128],
                                        qg_b[kh % 2][:, h4, qt * 128:(qt + 1) * 128],
                                        k_sb[:, kh, qt * 128 + 512:qt * 128 + 640],
                                        start=True, stop=True)
                        ins2.then_inc(s_pe.h, 1)

                def ptav_block(gq):
                    kh, qt = divmod(gq, 8)
                    for h4 in range(4):
                        gh = 4 * gq + h4
                        t.wait_ge(s_dv.h, dv_pn(gh))
                        if gh >= 2:
                            t.wait_ge(s_ac.h, ac_pt(gh - 2))
                        dstb = bank[4 + h4 % 2][:].bitcast(BF16)
                        for j in range(5):
                            ins = t.transpose(dstb[:, j * 128:(j + 1) * 128],
                                              pn_sb[gh % 8][:, j * 128:(j + 1) * 128],
                                              idb[:])
                        ins.then_inc(s_pe.h, 1)
                    t.wait_ge(s_ac.h, ac_pt(4 * gq + 3))
                    ob = 6 + qt % 2
                    if gq >= 2:
                        t.wait_ge(s_ac.h, ac_oev(gq - 2))
                    for j in range(5):
                        ins = t.matmul(bank[ob][:],
                                       v_tok[:, kh, qt + j, :],
                                       pt_sb[:, j, :, :].rearrange(
                                           "p a c -> p (a c)"),
                                       start=(j == 0), stop=(j == 4))
                    ins.then_inc(s_pe.h, 1)

                for gq in range(8 * NKV):
                    s_block(gq)
                    if gq >= 1:
                        ptav_block(gq - 1)
                ptav_block(8 * NKV - 1)
            block.tensor(f_pe)
            s_pe.n += PEQ * 8 * NKV


            def f_dve(v):
                for gq in range(8 * NKV):
                    kh, qt = divmod(gq, 8)
                    for h4 in range(4):
                        gh = 4 * gq + h4
                        par = h4
                        sb2 = h4 % 2
                        v.wait_ge(s_pe.h, pe_s(gq, h4))
                        if gh >= 4:
                            v.wait_ge(s_ac.h, ac_exp(gh - 4))
                        v.tensor_tensor(s_sb[par][:, 0:512], bank[2 * sb2][:],
                                        mask_sb[:, qt, 0:512], OP.add)
                        ins = v.tensor_tensor(s_sb[par][:, 512:640],
                                              bank[2 * sb2 + 1][:, 0:128],
                                              mask_sb[:, qt, 512:640], OP.add)
                        ins.then_inc(s_dv.h, 1)
                        v.wait_ge(s_dv.h, dv_mask(gh))
                        ins = v.reduce_max(MNEG[par], s_sb[par][:], axis=AX.X,
                                           negate=True)
                        ins.then_inc(s_dv.h, 1)
                        v.wait_ge(s_ac.h, ac_exp(gh))
                        if gh >= 4:
                            v.wait_ge(s_dv.h, dv_pn(gh - 4))
                        ins = v.reciprocal(RINV[par], SSUM[par])
                        ins.then_inc(s_dv.h, 1)
                        v.wait_ge(s_dv.h, dv0 + DVQ * gh + 3)
                        if gh >= 8:
                            v.wait_ge(s_pe.h, pe_pt(gh - 8))
                        ins = v.tensor_scalar(pn_sb[gh % 8][:], p_sb[par][:],
                                              RINV[par], None, OP.mult)
                        ins.then_inc(s_dv.h, 1)
            block.vector(f_dve)
            s_dv.n += DVQ * 4 * 8 * NKV

            def f_act(sc):
                def exp_block(gq):
                    for h4 in range(4):
                        gh = 4 * gq + h4
                        par = h4
                        sc.wait_ge(s_dv.h, dv_max(gh))
                        if gh >= 4:
                            sc.wait_ge(s_dv.h, dv_pn(gh - 4))
                        ins = sc.activation(p_sb[par][:], s_sb[par][:], AF.Exp,
                                            bias=MNEG[par], scale=1.0,
                                            accum_out=SSUM[par])
                        ins.then_inc(s_ac.h, 1)

                def ptev_block(gq):
                    kh, qt = divmod(gq, 8)
                    for h4 in range(4):
                        gh = 4 * gq + h4
                        sc.wait_ge(s_pe.h, pe_pt(gh))
                        if gq >= 1:
                            sc.wait_ge(s_pe.h, pe_av(gq - 1))
                        ins = sc.copy(pt_sb[:, :, h4, :],
                                      bank[4 + h4 % 2][:].bitcast(BF16)[:, 0:640]
                                      .rearrange("p (a c) -> p a c", a=5))
                        ins.then_inc(s_ac.h, 1)
                    sc.wait_ge(s_pe.h, pe_av(gq))
                    ins = sc.copy(obuf_res[:, 4 * kh:4 * kh + 4, qt * 128:(qt + 1) * 128],
                                  bank[6 + qt % 2][:].rearrange("p (a c) -> p a c",
                                                                a=4))
                    ins.then_inc(s_ac.h, 1)

                for gq in range(8 * NKV):
                    exp_block(gq)
                    if gq >= 1:
                        ptev_block(gq - 1)
                ptev_block(8 * NKV - 1)
            block.scalar(f_act)
            s_ac.n += ACQ * 8 * NKV



        def attn_debug_store():
            if obuf is None:
                return
            snap_ac = s_ac.n

            def f_dbg(sy):
                sy.wait_ge(s_ac.h, snap_ac)
                sy.dma_start(obuf.rearrange("h p q -> p h q"), obuf_res[:]
                             ).then_inc(s_st.h, 16)
            block.sync(f_dbg)
            s_st.n += 16

        def oproj_phase():
            wl0, pe0, ac0, dv0, hl0, st0 = (s_wl.n, s_pe.n, s_ac.n, s_dv.n,
                                            s_hl.n, s_st.n)
            n_tc = 2
            NT = 16 * n_tc
            order = []
            for dmc in range(16):
                order.append(("mm", dmc))
                if dmc >= 1:
                    order += [("tr", (dmc - 1) * n_tc + tc) for tc in range(n_tc)]
            order += [("tr", 15 * n_tc + tc) for tc in range(n_tc)]
            pe_at = {ev: pe0 + i + 1 for i, ev in enumerate(order)}

            def st3(g, g2):
                g.wait_ge(s_dv.h, dv0 + g2 + 1)
                g.dma_start(
                    yb[(g2 % n_tc) * 512:((g2 % n_tc) + 1) * 512,
                       (g2 // n_tc) * 256:((g2 // n_tc) + 1) * 256]
                    .rearrange("(a p) c -> p a c", p=128),
                    os3_b[g2 % 2][:]).then_inc(s_st.h, 16)

            def f_g(g):
                for dmc in range(16):
                    if dmc >= 2:
                        for g2 in range((dmc - 2) * n_tc, (dmc - 1) * n_tc):
                            st3(g, g2)
                    if dmc >= 2:
                        g.wait_ge(s_pe.h, pe_at[("mm", dmc - 2)])
                    g.dma_start(wo_b[dmc % 2][:], wo_t[dmc]).then_inc(s_wl.h, 16)
                for g2 in range(14 * n_tc, 16 * n_tc):
                    st3(g, g2)
            block.gpsimd(f_g)
            s_wl.n += 16 * 16
            s_st.n += 16 * NT

            def f_load(sy):
                for g2 in range(NT):
                    dmc, tc = divmod(g2, n_tc)
                    if g2 >= 6:
                        sy.wait_ge(s_dv.h, dv0 + g2 - 5)
                    sy.dma_start(
                        r3_b[g2 % 6][:],
                        ya[HALO + tc * 512:HALO + (tc + 1) * 512,
                           dmc * 256:(dmc + 1) * 256]
                        .rearrange("(a p) c -> p a c", p=128)).then_inc(s_hl.h, 16)
            block.sync(f_load)
            s_hl.n += 16 * NT

            def f_pe(t):
                for dmc in range(16):
                    t.wait_ge(s_wl.h, wl0 + 16 * (dmc + 1))
                    if dmc >= 1:
                        t.wait_ge(s_ac.h, ac0 + n_tc * dmc)
                    for fc in range(KC):
                        for dm2 in range(2):
                            for tc in range(n_tc):
                                ins = t.matmul(
                                    bank[dm2 * n_tc + tc][:],
                                    wo_b[dmc % 2][:, fc, dm2 * 128:(dm2 + 1) * 128],
                                    obuf_res[:, fc, tc * 512:(tc + 1) * 512],
                                    start=(fc == 0), stop=(fc == KC - 1))
                    ins.then_inc(s_pe.h, 1)
                    if dmc >= 1:
                        for tc in range(n_tc):
                            g2 = (dmc - 1) * n_tc + tc
                            t.wait_ge(s_ac.h, ac0 + g2 + 1)
                            if g2 >= 1:
                                t.wait_ge(s_dv.h, dv0 + g2)
                            for tt in range(4):
                                for dm2 in range(2):
                                    ins = t.transpose(
                                        bank[6 + dm2][:, tt * 128:(tt + 1) * 128],
                                        y3_b[g2 % 4][:, dm2, tt * 128:(tt + 1) * 128],
                                        idf[:])
                            ins.then_inc(s_pe.h, 1)
                for tc in range(n_tc):
                    g2 = 15 * n_tc + tc
                    t.wait_ge(s_ac.h, ac0 + g2 + 1)
                    t.wait_ge(s_dv.h, dv0 + g2)
                    for tt in range(4):
                        for dm2 in range(2):
                            ins = t.transpose(
                                bank[6 + dm2][:, tt * 128:(tt + 1) * 128],
                                y3_b[g2 % 4][:, dm2, tt * 128:(tt + 1) * 128],
                                idf[:])
                    ins.then_inc(s_pe.h, 1)
            block.tensor(f_pe)
            s_pe.n += len(order)

            def f_act(sc):
                for dmc in range(16):
                    sc.wait_ge(s_pe.h, pe_at[("mm", dmc)])
                    for tc in range(n_tc):
                        g2 = dmc * n_tc + tc
                        if g2 >= 4:
                            sc.wait_ge(s_pe.h, pe_at[("tr", g2 - 4)])
                        for dm2 in range(2):
                            ins = sc.copy(y3_b[g2 % 4][:, dm2, :],
                                          bank[dm2 * n_tc + tc][:])
                        ins.then_inc(s_ac.h, 1)
            block.scalar(f_act)
            s_ac.n += NT

            def f_dve(v):
                for g2 in range(NT):
                    v.wait_ge(s_pe.h, pe_at[("tr", g2)])
                    v.wait_ge(s_hl.h, hl0 + 16 * (g2 + 1))
                    if g2 >= 2:
                        v.wait_ge(s_st.h, st0 + 16 * (g2 - 1))
                    for tt in range(4):
                        for dm2 in range(2):
                            ins = v.tensor_tensor(
                                os3_b[g2 % 2][:, tt, dm2 * 128:(dm2 + 1) * 128],
                                bank[6 + dm2][:, tt * 128:(tt + 1) * 128],
                                r3_b[g2 % 6][:, tt, dm2 * 128:(dm2 + 1) * 128],
                                OP.add)
                    ins.then_inc(s_dv.h, 1)
            block.vector(f_dve)
            s_dv.n += NT

        if "A" in phases:
            with nc.named_scope("A0"):
                ln_phase(x_in, 8, xn_c)
                join_all()
            with nc.named_scope("A1"):
                ffn_up(xn_c, 2)
                join_all()
            with nc.named_scope("A2"):
                ffn_down(x_in, ya[HALO:HALO + T_OWN, :], 2)
                join_all()
            with nc.named_scope("AX"):
                def f_exch(g):
                    g.dma_start(cc_in, ya[T_OWN:T_OWN + HALO, :]).then_inc(s_hl.h, 16)
                    g.wait_ge(s_hl.h, s_hl.n + 16)
                    g.collective_compute(
                        "AllGather", OP.bypass,
                        replica_groups=[[0, 1, 2, 3, 4, 5, 6, 7]],
                        ins=[cc_in.opt()],
                        outs=[cc_out.opt()],
                    ).then_inc(s_wl.h)
                block.gpsimd(f_exch)
                s_hl.n += 16
                s_wl.n += 1

                def f_pick(sy):
                    sy.wait_ge(s_wl.h, s_wl.n)
                    pid = sy.partition_id()
                    idx = (pid + 7) % 8
                    cc3 = cc_out.rearrange("(a b) c -> a b c", b=HALO)
                    sy.dma_start(ya[0:HALO, :], cc3[idx]).then_inc(s_hl.h, 16)
                block.sync(f_pick)
                s_hl.n += 16
                join_all()

        bstg = set()
        for ph in phases:
            if ph == "B":
                bstg |= {"B0", "B1", "B2", "B3"}
            else:
                bstg.add(ph)
        if "B0" in bstg:
            with nc.named_scope("B0"):
                ln_phase(ya, 12, xn_ab)
                join_all()
        if "B1" in bstg:
            with nc.named_scope("B1"):
                proj_phase(xn_ab)
                join_all()
        if "B2" in bstg:
            with nc.named_scope("B2"):
                attn_phase()
                attn_debug_store()
                join_all()
        if "B3" in bstg:
            with nc.named_scope("B3"):
                oproj_phase()
                join_all()

        if "C" in phases:
            src_c = yb if "B3" in bstg else ya[0:T_OWN, :]
            with nc.named_scope("C0"):
                ln_phase(src_c, 8, xn_c)
                join_all()
            with nc.named_scope("C1"):
                ffn_up(xn_c, 2)
                join_all()
            with nc.named_scope("C2"):
                ffn_down(src_c, out, 2)
                join_all()

    return nc



def _prep_weights(wq, wk, wv, wo, wgate, wup, wdown):
    def blob(wT, n_out, chunk):
        return np.ascontiguousarray(
            wT.reshape(KC, 128, n_out // chunk, chunk).transpose(2, 1, 0, 3))

    wg_t = blob(np.ascontiguousarray(wgate.T), INTER, 256)
    wu_t = blob(np.ascontiguousarray(wup.T), INTER, 256)
    wdT = np.ascontiguousarray(wdown.T)
    wd_t = np.ascontiguousarray(wdT.reshape(86, 128, 8, 512).transpose(2, 1, 0, 3))
    wk_t = blob(np.ascontiguousarray(wk.T), NKV * HD, 128)
    wv_t = blob(np.ascontiguousarray(wv.T), NKV * HD, 128)
    wq_t = blob(np.ascontiguousarray(wq.T), NQ * HD, 128)
    wo_t = blob(np.ascontiguousarray(wo.T), D, 256)
    return dict(wg_t=wg_t, wu_t=wu_t, wd_t=wd_t, wk_t=wk_t, wv_t=wv_t,
                wq_t=wq_t, wo_t=wo_t)


def _masks(core):
    m = np.zeros((8, 128, 640), np.float32)
    j = np.arange(640)[None, :]
    qr = np.arange(128)[:, None]
    band = (j > qr) & (j <= qr + WIN)
    for qt in range(8):
        valid = band
        if core == 0:
            valid = valid & ((qt * 128 + j) >= HALO)
        m[qt] = np.where(valid, 0.0, NEG)
    return m


def kernel(L0_attn_x, L0_attn_wq, L0_attn_wk, L0_attn_wv, L0_attn_wo,
           L0_ffn_w_gate, L0_ffn_w_up, L0_ffn_w_down):
    x = np.asarray(L0_attn_x, np.float32).reshape(8192, D)
    wb = _prep_weights(np.asarray(L0_attn_wq, np.float32),
                       np.asarray(L0_attn_wk, np.float32),
                       np.asarray(L0_attn_wv, np.float32),
                       np.asarray(L0_attn_wo, np.float32),
                       np.asarray(L0_ffn_w_gate, np.float32),
                       np.asarray(L0_ffn_w_up, np.float32),
                       np.asarray(L0_ffn_w_down, np.float32))
    ident = np.eye(128, dtype=np.float32)

    nc = build_nc()
    in_maps = []
    for c in range(8):
        m = dict(wb)
        m["x_in"] = np.ascontiguousarray(x[c * T_OWN: (c + 1) * T_OWN])
        m["mask_in"] = _masks(c)
        m["ident_in"] = ident
        in_maps.append(m)
    res = run_bass_kernel_spmd(nc, in_maps, core_ids=list(range(8)))
    outp = np.concatenate([res.results[c]["out"] for c in range(8)], axis=0)
    return outp.reshape(1, 8192, D)


# version 07
# speedup vs baseline: 1.1067x; 1.0637x over previous
import os
import numpy as np
import ml_dtypes
import concourse.bass as bass
import concourse.mybir as mybir
from concourse.bass_utils import run_bass_kernel_spmd

F32 = mybir.dt.float32
BF16 = mybir.dt.bfloat16
U8 = mybir.dt.uint8
AF = mybir.ActivationFunctionType
AX = mybir.AxisListType
OP = mybir.AluOpType

D = 4096
KC = 32
INTER = 11008
NIC = 43
T_A = 1536
T_OWN = 1024
HALO = 512
NQ, NKV, HD, WIN = 32, 8, 128, 512
QK_SCALE = float(1.0 / np.sqrt(128.0))
NEG = -1.0e30
KB = 1024
f32c = np.float32
SEM_NAMES = {}


def _carve(arena, off, shape, dtype):
    nb = int(np.prod(shape[1:])) * mybir.dt.size(dtype)
    assert off % 32 == 0
    v = arena[:, off:off + nb].bitcast(dtype)
    if len(shape) > 2:
        dims = " ".join(f"d{i}" for i in range(1, len(shape)))
        v = v.rearrange(f"p ({dims}) -> p {dims}",
                        **{f"d{i}": shape[i] for i in range(1, len(shape))})
    return v, off + ((nb + 31) // 32) * 32


class Ctr:
    def __init__(self, h):
        self.h = h
        self.n = 0

    def inc(self, ins, k=1):
        ins.then_inc(self.h, k)
        self.n += k
        return self.n


def build_nc(phases=("A", "B", "C"), expose_ya=False, expose_yb=False,
             expose_q=False, expose_o=False):
    nc = bass.Bass("TRN2", target_bir_lowering=False)

    x_in = nc.dram_tensor("x_in", [T_OWN, D], F32, kind="ExternalInput").ap()
    wg_t = nc.dram_tensor("wg_t", [NIC, 128, KC, 256], F32, kind="ExternalInput").ap()
    wu_t = nc.dram_tensor("wu_t", [NIC, 128, KC, 256], F32, kind="ExternalInput").ap()
    wd_t = nc.dram_tensor("wd_t", [8, 128, 86, 512], F32, kind="ExternalInput").ap()
    wk_t = nc.dram_tensor("wk_t", [NKV, 128, KC, 128], F32, kind="ExternalInput").ap()
    wv_t = nc.dram_tensor("wv_t", [NKV, 128, KC, 128], F32, kind="ExternalInput").ap()
    wq_t = nc.dram_tensor("wq_t", [NQ, 128, KC, 128], F32, kind="ExternalInput").ap()
    wo_t = nc.dram_tensor("wo_t", [16, 128, KC, 256], F32, kind="ExternalInput").ap()
    mask_in = nc.dram_tensor("mask_in", [8, 128, 640], F32, kind="ExternalInput").ap()
    ident_in = nc.dram_tensor("ident_in", [128, 128], F32, kind="ExternalInput").ap()

    out = nc.dram_tensor("out", [T_OWN, D], F32, kind="ExternalOutput").ap()
    ya = nc.dram_tensor("ya", [T_A, D], F32,
                        kind="ExternalOutput" if expose_ya else "Internal").ap()
    yb = nc.dram_tensor("yb", [T_OWN, D], F32,
                        kind="ExternalOutput" if expose_yb else "Internal").ap()
    hbuf = nc.dram_tensor("hbuf", [86, 128, T_A], BF16, kind="Internal").ap()
    cc_in = nc.dram_tensor("cc_in", [HALO, D], F32, kind="Internal").ap()
    cc_out = nc.dram_tensor("cc_out", [8 * HALO, D], F32, kind="Internal",
                        addr_space="Shared").ap()
    qbuf = nc.dram_tensor("qbuf", [NQ, 128, T_OWN], BF16,
                          kind="ExternalOutput" if expose_q else "Internal").ap()
    obuf = (nc.dram_tensor("obuf", [NQ, 128, T_OWN], BF16,
                           kind="ExternalOutput").ap() if expose_o else None)

    from contextlib import ExitStack
    ctx = ExitStack()
    with ctx:
        arena = ctx.enter_context(nc.sbuf_tensor("arena", [128, 192 * KB], U8)).ap()
        bank = [ctx.enter_context(nc.psum_tensor(f"bank{i}", [128, 512], F32)).ap()
                for i in range(8)]

        o = 0
        idb, o = _carve(arena, o, [128, 128], BF16)
        idf, o = _carve(arena, o, [128, 128], F32)
        stats, o = _carve(arena, o, [128, 32], F32)
        assert o <= 1 * KB
        RAWS = stats[:, 0:1]
        MEAN = stats[:, 1:2]
        SUMSQ = stats[:, 2:3]
        VAR = stats[:, 3:4]
        M2 = stats[:, 4:5]
        RSTD_S = stats[:, 5:6]
        RSTD = stats[:, 6:7]
        MNEG = [stats[:, 8 + 4 * i:9 + 4 * i] for i in range(4)]
        SSUM = [stats[:, 9 + 4 * i:10 + 4 * i] for i in range(4)]
        RINV = [stats[:, 10 + 4 * i:11 + 4 * i] for i in range(4)]

        xn_ab, _ = _carve(arena, 1 * KB, [128, KC, T_A], BF16)
        xn_c, _ = _carve(arena, 1 * KB, [128, KC, T_OWN], BF16)
        wd_cache, _ = _carve(arena, 1 * KB, [128, 86, 512], BF16)
        o2 = 1 * KB
        qg_b = []
        for i in range(2):
            v, o2 = _carve(arena, o2, [128, 4, T_OWN], BF16)
            qg_b.append(v)
        obuf_res, o2 = _carve(arena, o2, [128, NQ, T_OWN], BF16)
        s_sb, p_sb, pn_sb = [], [], []
        for i in range(4):
            v, o2 = _carve(arena, o2, [128, 640], F32)
            s_sb.append(v)
        assert o2 <= 97 * KB, o2
        o2w = 145 * KB
        for i in range(4):
            v, o2w = _carve(arena, o2w, [128, 640], F32)
            p_sb.append(v)
        pt_sb, o2w = _carve(arena, o2w, [128, 5, 4, 128], BF16)
        assert o2w <= 161 * KB, o2w
        o2s = 161 * KB
        mask_sb, o2s = _carve(arena, o2s, [128, 8, 640], F32)
        for i in range(8):
            v, o2s = _carve(arena, o2s, [128, 640], BF16)
            pn_sb.append(v)
        assert o2s <= 192 * KB, o2s
        o3 = 97 * KB
        wo_b = []
        for i in range(2):
            v, o3 = _carve(arena, o3, [128, KC, 256], BF16)
            wo_b.append(v)
        y3_b = []
        for i in range(4):
            v, o3 = _carve(arena, o3, [128, 2, 512], F32)
            y3_b.append(v)
        r3_b = []
        for i in range(6):
            v, o3 = _carve(arena, o3, [128, 4, 256], F32)
            r3_b.append(v)
        os3_b = []
        for i in range(2):
            v, o3 = _carve(arena, o3, [128, 4, 256], F32)
            os3_b.append(v)
        assert o3 <= 184 * KB, o3

        W1 = 97 * KB
        o4 = W1
        wgb, wub = [], []
        for i in range(2):
            v, o4 = _carve(arena, o4, [128, KC, 256], BF16)
            wgb.append(v)
        for i in range(2):
            v, o4 = _carve(arena, o4, [128, KC, 256], BF16)
            wub.append(v)
        assert o4 <= 161 * KB
        o5 = W1
        trash, o5 = _carve(arena, o5, [128, D], F32)
        xt_b, xnb_b = [], []
        for i in range(2):
            v, o5 = _carve(arena, o5, [128, D], F32)
            xt_b.append(v)
        for i in range(2):
            v, o5 = _carve(arena, o5, [128, D], BF16)
            xnb_b.append(v)
        assert o5 <= 161 * KB
        o6 = W1
        hch_b = []
        for i in range(2):
            row = []
            for tc in range(2):
                v, o6 = _carve(arena, o6, [128, 8, 512], BF16)
                row.append(v)
            hch_b.append(row)
        y2_b = []
        for i in range(4):
            v, o6 = _carve(arena, o6, [128, 4, 512], F32)
            y2_b.append(v)
        assert o6 <= 161 * KB
        o6x = 1 * KB
        wdch_b = []
        for i in range(2):
            v, o6x = _carve(arena, o6x, [128, 8, 512], BF16)
            wdch_b.append(v)
        r2_b, os2_b = [], []
        for i in range(4):
            v, o6x = _carve(arena, o6x, [128, 4, 512], F32)
            r2_b.append(v)
        for i in range(2):
            v, o6x = _carve(arena, o6x, [128, 4, 512], F32)
            os2_b.append(v)
        assert o6x <= 97 * KB
        o7 = W1
        k_sb, o7 = _carve(arena, o7, [128, NKV, T_A], BF16)
        v_tok, o7 = _carve(arena, o7, [128, NKV, 12, 128], BF16)
        wpj_b = []
        for i in range(2):
            v, o7 = _carve(arena, o7, [128, KC, 128], BF16)
            wpj_b.append(v)
        assert o7 <= 161 * KB
        o9 = 161 * KB
        vf_sb, o9 = _carve(arena, o9, [128, T_A], BF16)
        qs_b = []
        for i in range(2):
            v, o9 = _carve(arena, o9, [128, 512], BF16)
            qs_b.append(v)
        gs_sb, o9 = _carve(arena, o9, [128, 3, 512], F32)
        hs_b = []
        for i in range(2):
            v, o9 = _carve(arena, o9, [128, 3, 512], BF16)
            hs_b.append(v)
        assert o9 <= 180 * KB, o9

        block = ctx.enter_context(nc.Block())

        s_wl = Ctr(ctx.enter_context(nc.semaphore("s_wl")))
        s_hl = Ctr(ctx.enter_context(nc.semaphore("s_hl")))
        s_st = Ctr(ctx.enter_context(nc.semaphore("s_st")))
        s_pe = Ctr(ctx.enter_context(nc.semaphore("s_pe")))
        s_ac = Ctr(ctx.enter_context(nc.semaphore("s_ac")))
        s_dv = Ctr(ctx.enter_context(nc.semaphore("s_dv")))
        ALL = [s_wl, s_hl, s_st, s_pe, s_ac, s_dv]
        global SEM_NAMES
        SEM_NAMES = {c.h.num: n for c, n in zip(ALL, ["wl", "hl", "st", "pe", "ac", "dv"])}

        def join_all():
            snap = [(c.h, c.n) for c in ALL]

            def wait_all(e):
                for h, n in snap:
                    if n:
                        e.wait_ge(h, n)
            block.sync(wait_all)
            block.gpsimd(wait_all)
            block.tensor(wait_all)
            block.scalar(wait_all)
            block.vector(wait_all)

        def p_pre_g(g):
            g.dma_start(idb[:], ident_in).then_inc(s_wl.h, 16)
        block.gpsimd(p_pre_g)
        s_wl.n += 16

        def p_pre_s(sy):
            sy.dma_start(idf[:], ident_in).then_inc(s_hl.h, 16)
        block.sync(p_pre_s)
        s_hl.n += 16
        join_all()

        def ln_phase(src, n_tiles, xn):
            hl0, ac0, dv0, pe0 = s_hl.n, s_ac.n, s_dv.n, s_pe.n
            DVP, ACP, PEP = 6, 6, 4

            def f_sync(sy):
                for tt in range(n_tiles):
                    if tt >= 2:
                        sy.wait_ge(s_dv.h, dv0 + DVP * (tt - 2) + 6)
                        sy.wait_ge(s_ac.h, ac0 + ACP * (tt - 2) + 1)
                    sy.dma_start(xt_b[tt % 2][:], src[tt * 128:(tt + 1) * 128, :]
                                 ).then_inc(s_hl.h, 16)
            block.sync(f_sync)
            s_hl.n += 16 * n_tiles

            def f_dve(v):
                for tt in range(n_tiles):
                    x_t = xt_b[tt % 2]
                    v.wait_ge(s_hl.h, hl0 + 16 * (tt + 1))
                    if tt >= 1:
                        v.wait_ge(s_dv.h, dv0 + DVP * tt)
                    ins = v.reduce_sum(RAWS, x_t[:], axis=AX.X)
                    ins.then_inc(s_dv.h, 1)
                    v.wait_ge(s_dv.h, dv0 + DVP * tt + 1)
                    ins = v.tensor_scalar_mul(MEAN, RAWS, 1.0 / D)
                    ins.then_inc(s_dv.h, 1)
                    v.wait_ge(s_ac.h, ac0 + ACP * tt + 1)
                    v.tensor_scalar(VAR, SUMSQ, 1.0 / D, None, OP.mult)
                    v.wait_ge(s_dv.h, dv0 + DVP * tt + 2)
                    ins = v.tensor_mul(M2, MEAN, MEAN)
                    ins.then_inc(s_dv.h, 1)
                    v.wait_ge(s_dv.h, dv0 + DVP * tt + 3)
                    ins = v.tensor_tensor(VAR, VAR, M2, OP.subtract)
                    ins.then_inc(s_dv.h, 1)
                    v.wait_ge(s_ac.h, ac0 + ACP * tt + 2)
                    ins = v.reciprocal(RSTD, RSTD_S)
                    ins.then_inc(s_dv.h, 1)
                    if tt >= 2:
                        v.wait_ge(s_pe.h, pe0 + PEP * (tt - 2) + 4)
                    v.wait_ge(s_dv.h, dv0 + DVP * tt + 5)
                    ins = v.tensor_scalar(xnb_b[tt % 2][:], x_t[:], MEAN, RSTD,
                                          OP.subtract, OP.mult)
                    ins.then_inc(s_dv.h, 1)
            block.vector(f_dve)
            s_dv.n += DVP * n_tiles

            def f_act(sc):
                for tt in range(n_tiles):
                    x_t = xt_b[tt % 2]
                    sc.wait_ge(s_hl.h, hl0 + 16 * (tt + 1))
                    if tt >= 1:
                        sc.wait_ge(s_dv.h, dv0 + DVP * (tt - 1) + 4)
                    ins = sc.activation(trash[:], x_t[:], AF.Square,
                                        bias=0.0, scale=1.0, accum_out=SUMSQ)
                    ins.then_inc(s_ac.h, 1)
                    sc.wait_ge(s_dv.h, dv0 + DVP * tt + 4)
                    ins = sc.activation(RSTD_S, VAR, AF.Sqrt, bias=f32c(1e-5),
                                        scale=1.0)
                    ins.then_inc(s_ac.h, 1)
                    for b in range(4):
                        sc.wait_ge(s_pe.h, pe0 + PEP * tt + b + 1)
                        src_ps = bank[6 + b % 2][:].bitcast(BF16)[:, 0:1024].rearrange(
                            "p (a c) -> p a c", a=8)
                        ins = sc.copy(xn[:, b * 8:(b + 1) * 8, tt * 128:(tt + 1) * 128],
                                      src_ps)
                        ins.then_inc(s_ac.h, 1)
            block.scalar(f_act)
            s_ac.n += ACP * n_tiles

            def f_pe(t):
                for tt in range(n_tiles):
                    t.wait_ge(s_dv.h, dv0 + DVP * tt + 6)
                    for b in range(4):
                        gb = tt * 4 + b
                        if gb >= 2:
                            tp, bp = divmod(gb - 2, 4)
                            t.wait_ge(s_ac.h, ac0 + ACP * tp + 3 + bp)
                        dst = bank[6 + b % 2][:].bitcast(BF16)
                        for j in range(8):
                            kc = b * 8 + j
                            ins = t.transpose(dst[:, j * 128:(j + 1) * 128],
                                              xnb_b[tt % 2][:, kc * 128:(kc + 1) * 128],
                                              idb[:])
                        ins.then_inc(s_pe.h, 1)
            block.tensor(f_pe)
            s_pe.n += PEP * n_tiles

        def ffn_up(xn, n_tc):
            wl0, pe0, ac0, dv0, st0 = s_wl.n, s_pe.n, s_ac.n, s_dv.n, s_st.n
            NG = 2 * NIC

            def f_g(g):
                for ic in range(NIC):
                    if ic >= 2:
                        g.wait_ge(s_pe.h, pe0 + 4 * ic - 4)
                    g.dma_start(wgb[ic % 2][:], wg_t[ic]).then_inc(s_wl.h, 16)
                    g.dma_start(wub[ic % 2][:], wu_t[ic]).then_inc(s_wl.h, 16)
            block.gpsimd(f_g)
            s_wl.n += 32 * NIC

            def f_pe(t):
                for ic in range(NIC):
                    t.wait_ge(s_wl.h, wl0 + 32 * (ic + 1))
                    for i2 in range(2):
                        gi = 2 * ic + i2
                        if gi >= 1:
                            t.wait_ge(s_ac.h, ac0 + gi)
                        for kc in range(KC):
                            for tc in range(n_tc):
                                ins = t.matmul(bank[tc][:],
                                               wgb[ic % 2][:, kc, i2 * 128:(i2 + 1) * 128],
                                               xn[:, kc, tc * 512:(tc + 1) * 512],
                                               start=(kc == 0), stop=(kc == KC - 1))
                        ins.then_inc(s_pe.h, 1)
                        if gi >= 1:
                            t.wait_ge(s_dv.h, dv0 + gi)
                        for kc in range(KC):
                            for tc in range(n_tc):
                                ins = t.matmul(bank[3 + tc][:],
                                               wub[ic % 2][:, kc, i2 * 128:(i2 + 1) * 128],
                                               xn[:, kc, tc * 512:(tc + 1) * 512],
                                               start=(kc == 0), stop=(kc == KC - 1))
                        ins.then_inc(s_pe.h, 1)
            block.tensor(f_pe)
            s_pe.n += 2 * NG

            def f_act(sc):
                for gi in range(NG):
                    sc.wait_ge(s_pe.h, pe0 + 2 * gi + 1)
                    if gi >= 1:
                        sc.wait_ge(s_dv.h, dv0 + gi)
                    for tc in range(n_tc):
                        ins = sc.activation(gs_sb[:, tc, :], bank[tc][:], AF.Silu,
                                            bias=0.0, scale=1.0)
                    ins.then_inc(s_ac.h, 1)
            block.scalar(f_act)
            s_ac.n += NG

            def f_dve(v):
                for gi in range(NG):
                    v.wait_ge(s_pe.h, pe0 + 2 * gi + 2)
                    v.wait_ge(s_ac.h, ac0 + gi + 1)
                    if gi >= 2:
                        v.wait_ge(s_st.h, st0 + 16 * (gi - 1))
                    for tc in range(n_tc):
                        ins = v.scalar_tensor_tensor(hs_b[gi % 2][:, tc, :],
                                                     bank[3 + tc][:], 0.5,
                                                     gs_sb[:, tc, :],
                                                     OP.mult, OP.mult)
                    ins.then_inc(s_dv.h, 1)
            block.vector(f_dve)
            s_dv.n += NG

            def f_store(sy):
                for gi in range(NG):
                    sy.wait_ge(s_dv.h, dv0 + gi + 1)
                    sy.dma_start(hbuf[gi, :, 0:n_tc * 512],
                                 hs_b[gi % 2][:, 0:n_tc, :].rearrange("p a c -> p (a c)")
                                 ).then_inc(s_st.h, 16)
            block.sync(f_store)
            s_st.n += 16 * NG

        def ffn_down(res_src, dst, n_tc):
            assert n_tc == 2
            wl0, pe0, ac0, dv0, hl0, st0 = (s_wl.n, s_pe.n, s_ac.n, s_dv.n,
                                            s_hl.n, s_st.n)
            NB = 11
            NT = 16

            def pe_oct(dmc, j):
                return pe0 + 13 * dmc + j + 1

            def pe_mm(dmc):
                return pe_oct(dmc, NB - 1)

            def pe_tr(g2):
                return pe0 + 13 * (g2 // 2) + NB + (g2 % 2) + 1

            def f_g(g):
                for dmc in range(8):
                    if dmc >= 1:
                        for g2 in range(2 * (dmc - 1), 2 * dmc):
                            g.wait_ge(s_dv.h, dv0 + 16 * (g2 + 1))
                            g.dma_start(
                                dst[(g2 % 2) * 512:((g2 % 2) + 1) * 512,
                                    (g2 // 2) * 512:((g2 // 2) + 1) * 512]
                                .rearrange("(a p) c -> p a c", p=128),
                                os2_b[g2 % 2][:]).then_inc(s_st.h, 16)
                    for j in range(NB):
                        go = dmc * NB + j
                        if go >= 2:
                            g.wait_ge(s_pe.h, pe_oct((go - 2) // NB, (go - 2) % NB))
                        nic8 = 8 if j < 10 else 6
                        g.dma_start(wdch_b[go % 2][:, 0:nic8, :],
                                    wd_t[dmc][:, j * 8:j * 8 + nic8, :]
                                    ).then_inc(s_wl.h, 16)
                for g2 in range(14, 16):
                    g.wait_ge(s_dv.h, dv0 + 16 * (g2 + 1))
                    g.dma_start(
                        dst[(g2 % 2) * 512:((g2 % 2) + 1) * 512,
                            (g2 // 2) * 512:((g2 // 2) + 1) * 512]
                        .rearrange("(a p) c -> p a c", p=128),
                        os2_b[g2 % 2][:]).then_inc(s_st.h, 16)
            block.gpsimd(f_g)
            s_wl.n += 16 * 8 * NB
            s_st.n += 16 * NT

            def hl_at(dmc, j, tc):
                return hl0 + 16 * (24 * dmc + 2 + 2 * j + tc + 1)

            def f_load(sy):
                for dmc in range(8):
                    for tc in range(2):
                        g2 = 2 * dmc + tc
                        if g2 >= 4:
                            sy.wait_ge(s_dv.h, dv0 + 16 * (g2 - 3))
                        sy.dma_start(
                            r2_b[g2 % 4][:],
                            res_src[tc * 512:(tc + 1) * 512,
                                    dmc * 512:(dmc + 1) * 512]
                            .rearrange("(a p) c -> p a c", p=128)
                            ).then_inc(s_hl.h, 16)
                    for j in range(NB):
                        go = dmc * NB + j
                        if go >= 2:
                            sy.wait_ge(s_pe.h, pe_oct((go - 2) // NB, (go - 2) % NB))
                        nic8 = 8 if j < 10 else 6
                        for tc in range(2):
                            sy.dma_start(
                                hch_b[go % 2][tc][:, 0:nic8, :],
                                hbuf[j * 8:j * 8 + nic8, :, tc * 512:(tc + 1) * 512]
                                .rearrange("a p c -> p a c")).then_inc(s_hl.h, 16)
            block.sync(f_load)
            s_hl.n += 16 * 8 * 24

            def f_pe(t):
                for dmc in range(8):
                    if dmc >= 1:
                        t.wait_ge(s_dv.h, dv0 + 16 * 2 * dmc)
                    for j in range(NB):
                        go = dmc * NB + j
                        t.wait_ge(s_wl.h, wl0 + 16 * (go + 1))
                        t.wait_ge(s_hl.h, hl_at(dmc, j, 1))
                        nic8 = 8 if j < 10 else 6
                        for i8 in range(nic8):
                            ic = j * 8 + i8
                            for dm2 in range(4):
                                for tc in range(2):
                                    ins = t.matmul(
                                        bank[dm2 * 2 + tc][:],
                                        wdch_b[go % 2][:, i8, dm2 * 128:(dm2 + 1) * 128],
                                        hch_b[go % 2][tc][:, i8, :],
                                        start=(ic == 0), stop=(ic == 85))
                        ins.then_inc(s_pe.h, 1)
                    for tc in range(2):
                        g2 = 2 * dmc + tc
                        t.wait_ge(s_ac.h, ac0 + g2 + 1)
                        for tt in range(4):
                            for dm2 in range(4):
                                ins = t.transpose(
                                    bank[dm2 * 2 + tc][:, tt * 128:(tt + 1) * 128],
                                    y2_b[g2 % 4][:, dm2, tt * 128:(tt + 1) * 128],
                                    idf[:])
                        ins.then_inc(s_pe.h, 1)
            block.tensor(f_pe)
            s_pe.n += 13 * 8

            def f_act(sc):
                for dmc in range(8):
                    for tc in range(2):
                        g2 = 2 * dmc + tc
                        sc.wait_ge(s_pe.h, pe_mm(dmc))
                        if g2 >= 4:
                            sc.wait_ge(s_pe.h, pe_tr(g2 - 4))
                        for dm2 in range(4):
                            ins = sc.copy(y2_b[g2 % 4][:, dm2, :],
                                          bank[dm2 * 2 + tc][:])
                        ins.then_inc(s_ac.h, 1)
            block.scalar(f_act)
            s_ac.n += NT

            def f_dve(v):
                for g2 in range(NT):
                    dmc, tc = divmod(g2, 2)
                    v.wait_ge(s_pe.h, pe_tr(g2))
                    v.wait_ge(s_hl.h, hl0 + 16 * (24 * dmc + tc + 1))
                    if g2 >= 2:
                        v.wait_ge(s_st.h, st0 + 16 * (g2 - 1))
                    for tt in range(4):
                        for dm2 in range(4):
                            ins = v.tensor_tensor(
                                os2_b[g2 % 2][:, tt, dm2 * 128:(dm2 + 1) * 128],
                                bank[dm2 * 2 + tc][:, tt * 128:(tt + 1) * 128],
                                r2_b[g2 % 4][:, tt, dm2 * 128:(dm2 + 1) * 128],
                                OP.add)
                    ins.then_inc(s_dv.h, 16)
            block.vector(f_dve)
            s_dv.n += 16 * NT

        def proj_phase(xn):
            wl0, pe0, ac0, dv0, st0 = s_wl.n, s_pe.n, s_ac.n, s_dv.n, s_st.n

            def pe_k(kh):
                return pe0 + kh + 1

            def pe_vp(kh):
                return pe0 + NKV + 2 * kh + 1

            def pe_vt(kh):
                return pe0 + NKV + 2 * kh + 2

            def pe_q(h):
                return pe0 + 3 * NKV + h + 1

            def ac_k(kh):
                return ac0 + kh + 1

            def ac_vf(kh):
                return ac0 + NKV + 3 * kh + 1

            def ac_vt(kh, i):
                return ac0 + NKV + 3 * kh + 2 + i

            def ac_q(h, tc):
                return ac0 + 4 * NKV + 2 * h + tc + 1

            def f_w(g):
                for kh in range(NKV):
                    if kh >= 2:
                        g.wait_ge(s_pe.h, pe_k(kh - 2))
                    g.dma_start(wpj_b[kh % 2][:], wk_t[kh]).then_inc(s_wl.h, 16)
                for kh in range(NKV):
                    g.wait_ge(s_pe.h, pe_k(6 + kh % 2) if kh < 2
                              else pe_vt(kh - 2))
                    g.dma_start(wpj_b[kh % 2][:], wv_t[kh]).then_inc(s_wl.h, 16)
                for h in range(NQ):
                    g.wait_ge(s_pe.h, pe_vt(6 + h % 2) if h < 2
                              else pe_q(h - 2))
                    g.dma_start(wpj_b[h % 2][:], wq_t[h]).then_inc(s_wl.h, 16)
            block.gpsimd(f_w)
            s_wl.n += 16 * (2 * NKV + NQ)

            def f_pe(t):
                for kh in range(NKV):
                    t.wait_ge(s_wl.h, wl0 + 16 * (kh + 1))
                    if kh >= 1:
                        t.wait_ge(s_ac.h, ac_k(kh - 1))
                    for kc in range(KC):
                        for tc in range(3):
                            ins = t.matmul(bank[tc][:],
                                           wpj_b[kh % 2][:, kc, :],
                                           xn[:, kc, tc * 512:(tc + 1) * 512],
                                           start=(kc == 0), stop=(kc == KC - 1))
                    ins.then_inc(s_pe.h, 1)
                for kh in range(NKV):
                    t.wait_ge(s_wl.h, wl0 + 16 * NKV + 16 * (kh + 1))
                    t.wait_ge(s_ac.h, ac_vf(kh - 1) if kh >= 1 else ac_k(NKV - 1))
                    for kc in range(KC):
                        for tc in range(3):
                            ins = t.matmul(bank[tc][:],
                                           wpj_b[kh % 2][:, kc, :],
                                           xn[:, kc, tc * 512:(tc + 1) * 512],
                                           start=(kc == 0), stop=(kc == KC - 1))
                    ins.then_inc(s_pe.h, 1)
                    t.wait_ge(s_ac.h, ac_vf(kh))
                    if kh >= 1:
                        t.wait_ge(s_ac.h, ac_vt(kh - 1, 1))
                    for tt in range(12):
                        dstb = bank[3 + tt // 8][:].bitcast(BF16)
                        ins = t.transpose(dstb[:, (tt % 8) * 128:(tt % 8 + 1) * 128],
                                          vf_sb[:, tt * 128:(tt + 1) * 128], idb[:])
                    ins.then_inc(s_pe.h, 1)
                for h in range(NQ):
                    t.wait_ge(s_wl.h, wl0 + 32 * NKV + 16 * (h + 1))
                    if h >= 1:
                        t.wait_ge(s_ac.h, ac_q(h - 1, 1))
                    for kc in range(KC):
                        for tc in range(2):
                            ins = t.matmul(bank[5 + tc][:],
                                           wpj_b[h % 2][:, kc, :],
                                           xn[:, kc, HALO + tc * 512:HALO + (tc + 1) * 512],
                                           start=(kc == 0), stop=(kc == KC - 1))
                    ins.then_inc(s_pe.h, 1)
            block.tensor(f_pe)
            s_pe.n += 3 * NKV + NQ

            def f_act(sc):
                for kh in range(NKV):
                    sc.wait_ge(s_pe.h, pe_k(kh))
                    for tc in range(3):
                        ins = sc.copy(k_sb[:, kh, tc * 512:(tc + 1) * 512],
                                      bank[tc][:])
                    ins.then_inc(s_ac.h, 1)
                for kh in range(NKV):
                    sc.wait_ge(s_pe.h, pe_vp(kh))
                    if kh >= 1:
                        sc.wait_ge(s_pe.h, pe_vt(kh - 1))
                    for tc in range(3):
                        ins = sc.copy(vf_sb[:, tc * 512:(tc + 1) * 512], bank[tc][:])
                    ins.then_inc(s_ac.h, 1)
                    sc.wait_ge(s_pe.h, pe_vt(kh))
                    ins = sc.copy(v_tok[:, kh, 0:8, :],
                                  bank[3][:].bitcast(BF16)[:, 0:1024].rearrange(
                                      "p (a c) -> p a c", a=8))
                    ins.then_inc(s_ac.h, 1)
                    ins = sc.copy(v_tok[:, kh, 8:12, :],
                                  bank[4][:].bitcast(BF16)[:, 0:512].rearrange(
                                      "p (a c) -> p a c", a=4))
                    ins.then_inc(s_ac.h, 1)
                for h in range(NQ):
                    sc.wait_ge(s_pe.h, pe_q(h))
                    for tc in range(2):
                        qi = 2 * h + tc
                        if qi >= 2:
                            sc.wait_ge(s_st.h, st0 + 16 * (qi - 1))
                        ins = sc.activation(qs_b[qi % 2][:], bank[5 + tc][:],
                                            AF.Copy, bias=0.0, scale=f32c(QK_SCALE))
                        ins.then_inc(s_ac.h, 1)
            block.scalar(f_act)
            s_ac.n += 4 * NKV + 2 * NQ

            def f_qstore(sy):
                for h in range(NQ):
                    for tc in range(2):
                        sy.wait_ge(s_ac.h, ac_q(h, tc))
                        sy.dma_start(qbuf[h, :, tc * 512:(tc + 1) * 512],
                                     qs_b[(2 * h + tc) % 2][:]).then_inc(s_st.h, 16)
            block.sync(f_qstore)
            s_st.n += 16 * 2 * NQ

        def attn_phase():
            pe0, ac0, dv0, hl0, st0 = s_pe.n, s_ac.n, s_dv.n, s_hl.n, s_st.n
            PEQ, ACQ, DVQ = 9, 9, 4

            NGQ = 8 * NKV

            def pe_s(gq, h4):
                return pe0 + 4 * gq + 5 * max(0, gq - 1) + h4 + 1

            def pe_pt(gh):
                gq, h4 = divmod(gh, 4)
                ns = 4 * min(gq + 2, NGQ)
                return pe0 + ns + 5 * gq + h4 + 1

            def pe_av(gq):
                ns = 4 * min(gq + 2, NGQ)
                return pe0 + ns + 5 * gq + 5

            def ac_exp(gh):
                gq = gh // 4
                return ac0 + 4 * gq + 5 * max(0, gq - 1) + gh % 4 + 1

            def ac_pt(gh):
                gq, h4 = divmod(gh, 4)
                ns = 4 * min(gq + 2, 8 * NKV)
                return ac0 + ns + 5 * gq + h4 + 1

            def ac_oev(gq):
                ns = 4 * min(gq + 2, 8 * NKV)
                return ac0 + ns + 5 * gq + 5

            def dv_mask(gh):
                gq, h4 = divmod(gh, 4)
                return dv0 + 16 * gq + 2 * h4 + 1

            def dv_max(gh):
                gq, h4 = divmod(gh, 4)
                return dv0 + 16 * gq + 2 * h4 + 2

            def dv_rec(gh):
                gq, h4 = divmod(gh, 4)
                return dv0 + 16 * gq + 8 + 2 * h4 + 1

            def dv_pn(gh):
                gq, h4 = divmod(gh, 4)
                return dv0 + 16 * gq + 8 + 2 * h4 + 2

            def f_qload(sy):
                sy.dma_start(mask_sb[:], mask_in.rearrange("q p k -> p q k")
                             ).then_inc(s_hl.h, 16)
                for kh in range(NKV):
                    if kh >= 2:
                        sy.wait_ge(s_pe.h, pe_s(8 * (kh - 2) + 7, 3))
                    sy.dma_start(qg_b[kh % 2][:],
                                 qbuf[4 * kh:4 * kh + 4].rearrange("h p q -> p h q")
                                 ).then_inc(s_hl.h, 16)
            block.sync(f_qload)
            s_hl.n += 16 * (NKV + 1)

            def f_pe(t):
                def s_block(gq):
                    kh, qt = divmod(gq, 8)
                    if qt == 0:
                        t.wait_ge(s_hl.h, hl0 + 16 * (kh + 2))
                    for h4 in range(4):
                        gh = 4 * gq + h4
                        par = h4 % 2
                        if gh >= 2:
                            t.wait_ge(s_dv.h, dv_mask(gh - 2))
                        ins = t.matmul(bank[2 * par][:],
                                       qg_b[kh % 2][:, h4, qt * 128:(qt + 1) * 128],
                                       k_sb[:, kh, qt * 128:qt * 128 + 512],
                                       start=True, stop=True)
                        ins2 = t.matmul(bank[2 * par + 1][:, 0:128],
                                        qg_b[kh % 2][:, h4, qt * 128:(qt + 1) * 128],
                                        k_sb[:, kh, qt * 128 + 512:qt * 128 + 640],
                                        start=True, stop=True)
                        ins2.then_inc(s_pe.h, 1)

                def ptav_block(gq):
                    kh, qt = divmod(gq, 8)
                    for h4 in range(4):
                        gh = 4 * gq + h4
                        t.wait_ge(s_dv.h, dv_pn(gh))
                        if gh >= 2:
                            t.wait_ge(s_ac.h, ac_pt(gh - 2))
                        dstb = bank[4 + h4 % 2][:].bitcast(BF16)
                        for j in range(5):
                            ins = t.transpose(dstb[:, j * 128:(j + 1) * 128],
                                              pn_sb[gh % 8][:, j * 128:(j + 1) * 128],
                                              idb[:])
                        ins.then_inc(s_pe.h, 1)
                    t.wait_ge(s_ac.h, ac_pt(4 * gq + 3))
                    ob = 6 + qt % 2
                    if gq >= 2:
                        t.wait_ge(s_ac.h, ac_oev(gq - 2))
                    for j in range(5):
                        ins = t.matmul(bank[ob][:],
                                       v_tok[:, kh, qt + j, :],
                                       pt_sb[:, j, :, :].rearrange(
                                           "p a c -> p (a c)"),
                                       start=(j == 0), stop=(j == 4))
                    ins.then_inc(s_pe.h, 1)

                for gq in range(8 * NKV):
                    s_block(gq)
                    if gq >= 1:
                        ptav_block(gq - 1)
                ptav_block(8 * NKV - 1)
            block.tensor(f_pe)
            s_pe.n += PEQ * 8 * NKV


            def f_dve(v):
                for gq in range(8 * NKV):
                    kh, qt = divmod(gq, 8)
                    for h4 in range(4):
                        gh = 4 * gq + h4
                        par = h4
                        sb2 = h4 % 2
                        v.wait_ge(s_pe.h, pe_s(gq, h4))
                        if gh >= 4:
                            v.wait_ge(s_ac.h, ac_exp(gh - 4))
                        v.tensor_tensor(s_sb[par][:, 0:512], bank[2 * sb2][:],
                                        mask_sb[:, qt, 0:512], OP.add)
                        ins = v.tensor_tensor(s_sb[par][:, 512:640],
                                              bank[2 * sb2 + 1][:, 0:128],
                                              mask_sb[:, qt, 512:640], OP.add)
                        ins.then_inc(s_dv.h, 1)
                        v.wait_ge(s_dv.h, dv_mask(gh))
                        ins = v.reduce_max(MNEG[par], s_sb[par][:], axis=AX.X,
                                           negate=True)
                        ins.then_inc(s_dv.h, 1)
                    for h4 in range(4):
                        gh = 4 * gq + h4
                        par = h4
                        v.wait_ge(s_ac.h, ac_exp(gh))
                        if gh >= 4:
                            v.wait_ge(s_dv.h, dv_pn(gh - 4))
                        ins = v.reciprocal(RINV[par], SSUM[par])
                        ins.then_inc(s_dv.h, 1)
                        v.wait_ge(s_dv.h, dv_rec(gh))
                        if gh >= 8:
                            v.wait_ge(s_pe.h, pe_pt(gh - 8))
                        ins = v.tensor_scalar(pn_sb[gh % 8][:], p_sb[par][:],
                                              RINV[par], None, OP.mult)
                        ins.then_inc(s_dv.h, 1)
            block.vector(f_dve)
            s_dv.n += DVQ * 4 * 8 * NKV

            def f_act(sc):
                def exp_block(gq):
                    for h4 in range(4):
                        gh = 4 * gq + h4
                        par = h4
                        sc.wait_ge(s_dv.h, dv_max(gh))
                        if gh >= 4:
                            sc.wait_ge(s_dv.h, dv_pn(gh - 4))
                        ins = sc.activation(p_sb[par][:], s_sb[par][:], AF.Exp,
                                            bias=MNEG[par], scale=1.0,
                                            accum_out=SSUM[par])
                        ins.then_inc(s_ac.h, 1)

                def ptev_block(gq):
                    kh, qt = divmod(gq, 8)
                    for h4 in range(4):
                        gh = 4 * gq + h4
                        sc.wait_ge(s_pe.h, pe_pt(gh))
                        if gq >= 1:
                            sc.wait_ge(s_pe.h, pe_av(gq - 1))
                        ins = sc.copy(pt_sb[:, :, h4, :],
                                      bank[4 + h4 % 2][:].bitcast(BF16)[:, 0:640]
                                      .rearrange("p (a c) -> p a c", a=5))
                        ins.then_inc(s_ac.h, 1)
                    sc.wait_ge(s_pe.h, pe_av(gq))
                    ins = sc.copy(obuf_res[:, 4 * kh:4 * kh + 4, qt * 128:(qt + 1) * 128],
                                  bank[6 + qt % 2][:].rearrange("p (a c) -> p a c",
                                                                a=4))
                    ins.then_inc(s_ac.h, 1)

                for gq in range(8 * NKV):
                    exp_block(gq)
                    if gq >= 1:
                        ptev_block(gq - 1)
                ptev_block(8 * NKV - 1)
            block.scalar(f_act)
            s_ac.n += ACQ * 8 * NKV



        def attn_debug_store():
            if obuf is None:
                return
            snap_ac = s_ac.n

            def f_dbg(sy):
                sy.wait_ge(s_ac.h, snap_ac)
                sy.dma_start(obuf.rearrange("h p q -> p h q"), obuf_res[:]
                             ).then_inc(s_st.h, 16)
            block.sync(f_dbg)
            s_st.n += 16

        def oproj_phase():
            wl0, pe0, ac0, dv0, hl0, st0 = (s_wl.n, s_pe.n, s_ac.n, s_dv.n,
                                            s_hl.n, s_st.n)
            n_tc = 2
            NT = 16 * n_tc
            order = []
            for dmc in range(16):
                order.append(("mm", dmc))
                if dmc >= 1:
                    order += [("tr", (dmc - 1) * n_tc + tc) for tc in range(n_tc)]
            order += [("tr", 15 * n_tc + tc) for tc in range(n_tc)]
            pe_at = {ev: pe0 + i + 1 for i, ev in enumerate(order)}

            def st3(g, g2):
                g.wait_ge(s_dv.h, dv0 + g2 + 1)
                g.dma_start(
                    yb[(g2 % n_tc) * 512:((g2 % n_tc) + 1) * 512,
                       (g2 // n_tc) * 256:((g2 // n_tc) + 1) * 256]
                    .rearrange("(a p) c -> p a c", p=128),
                    os3_b[g2 % 2][:]).then_inc(s_st.h, 16)

            def f_g(g):
                for dmc in range(16):
                    if dmc >= 2:
                        for g2 in range((dmc - 2) * n_tc, (dmc - 1) * n_tc):
                            st3(g, g2)
                    if dmc >= 2:
                        g.wait_ge(s_pe.h, pe_at[("mm", dmc - 2)])
                    g.dma_start(wo_b[dmc % 2][:], wo_t[dmc]).then_inc(s_wl.h, 16)
                for g2 in range(14 * n_tc, 16 * n_tc):
                    st3(g, g2)
            block.gpsimd(f_g)
            s_wl.n += 16 * 16
            s_st.n += 16 * NT

            def f_load(sy):
                for g2 in range(NT):
                    dmc, tc = divmod(g2, n_tc)
                    if g2 >= 6:
                        sy.wait_ge(s_dv.h, dv0 + g2 - 5)
                    sy.dma_start(
                        r3_b[g2 % 6][:],
                        ya[HALO + tc * 512:HALO + (tc + 1) * 512,
                           dmc * 256:(dmc + 1) * 256]
                        .rearrange("(a p) c -> p a c", p=128)).then_inc(s_hl.h, 16)
            block.sync(f_load)
            s_hl.n += 16 * NT

            def f_pe(t):
                for dmc in range(16):
                    t.wait_ge(s_wl.h, wl0 + 16 * (dmc + 1))
                    if dmc >= 1:
                        t.wait_ge(s_ac.h, ac0 + n_tc * dmc)
                    for fc in range(KC):
                        for dm2 in range(2):
                            for tc in range(n_tc):
                                ins = t.matmul(
                                    bank[dm2 * n_tc + tc][:],
                                    wo_b[dmc % 2][:, fc, dm2 * 128:(dm2 + 1) * 128],
                                    obuf_res[:, fc, tc * 512:(tc + 1) * 512],
                                    start=(fc == 0), stop=(fc == KC - 1))
                    ins.then_inc(s_pe.h, 1)
                    if dmc >= 1:
                        for tc in range(n_tc):
                            g2 = (dmc - 1) * n_tc + tc
                            t.wait_ge(s_ac.h, ac0 + g2 + 1)
                            if g2 >= 1:
                                t.wait_ge(s_dv.h, dv0 + g2)
                            for tt in range(4):
                                for dm2 in range(2):
                                    ins = t.transpose(
                                        bank[6 + dm2][:, tt * 128:(tt + 1) * 128],
                                        y3_b[g2 % 4][:, dm2, tt * 128:(tt + 1) * 128],
                                        idf[:])
                            ins.then_inc(s_pe.h, 1)
                for tc in range(n_tc):
                    g2 = 15 * n_tc + tc
                    t.wait_ge(s_ac.h, ac0 + g2 + 1)
                    t.wait_ge(s_dv.h, dv0 + g2)
                    for tt in range(4):
                        for dm2 in range(2):
                            ins = t.transpose(
                                bank[6 + dm2][:, tt * 128:(tt + 1) * 128],
                                y3_b[g2 % 4][:, dm2, tt * 128:(tt + 1) * 128],
                                idf[:])
                    ins.then_inc(s_pe.h, 1)
            block.tensor(f_pe)
            s_pe.n += len(order)

            def f_act(sc):
                for dmc in range(16):
                    sc.wait_ge(s_pe.h, pe_at[("mm", dmc)])
                    for tc in range(n_tc):
                        g2 = dmc * n_tc + tc
                        if g2 >= 4:
                            sc.wait_ge(s_pe.h, pe_at[("tr", g2 - 4)])
                        for dm2 in range(2):
                            ins = sc.copy(y3_b[g2 % 4][:, dm2, :],
                                          bank[dm2 * n_tc + tc][:])
                        ins.then_inc(s_ac.h, 1)
            block.scalar(f_act)
            s_ac.n += NT

            def f_dve(v):
                for g2 in range(NT):
                    v.wait_ge(s_pe.h, pe_at[("tr", g2)])
                    v.wait_ge(s_hl.h, hl0 + 16 * (g2 + 1))
                    if g2 >= 2:
                        v.wait_ge(s_st.h, st0 + 16 * (g2 - 1))
                    for tt in range(4):
                        for dm2 in range(2):
                            ins = v.tensor_tensor(
                                os3_b[g2 % 2][:, tt, dm2 * 128:(dm2 + 1) * 128],
                                bank[6 + dm2][:, tt * 128:(tt + 1) * 128],
                                r3_b[g2 % 6][:, tt, dm2 * 128:(dm2 + 1) * 128],
                                OP.add)
                    ins.then_inc(s_dv.h, 1)
            block.vector(f_dve)
            s_dv.n += NT

        if "A" in phases:
            with nc.named_scope("A0"):
                ln_phase(x_in, 8, xn_c)
                join_all()
            with nc.named_scope("A1"):
                ffn_up(xn_c, 2)
                join_all()
            with nc.named_scope("A2"):
                ffn_down(x_in, ya[HALO:HALO + T_OWN, :], 2)
                join_all()
            with nc.named_scope("AX"):
                def f_exch(g):
                    g.dma_start(cc_in, ya[T_OWN:T_OWN + HALO, :]).then_inc(s_hl.h, 16)
                    g.wait_ge(s_hl.h, s_hl.n + 16)
                    g.collective_compute(
                        "AllGather", OP.bypass,
                        replica_groups=[[0, 1, 2, 3, 4, 5, 6, 7]],
                        ins=[cc_in.opt()],
                        outs=[cc_out.opt()],
                    ).then_inc(s_wl.h)
                block.gpsimd(f_exch)
                s_hl.n += 16
                s_wl.n += 1

                def f_pick(sy):
                    sy.wait_ge(s_wl.h, s_wl.n)
                    pid = sy.partition_id()
                    idx = (pid + 7) % 8
                    cc3 = cc_out.rearrange("(a b) c -> a b c", b=HALO)
                    sy.dma_start(ya[0:HALO, :], cc3[idx]).then_inc(s_hl.h, 16)
                block.sync(f_pick)
                s_hl.n += 16
                join_all()

        bstg = set()
        for ph in phases:
            if ph == "B":
                bstg |= {"B0", "B1", "B2", "B3"}
            else:
                bstg.add(ph)
        if "B0" in bstg:
            with nc.named_scope("B0"):
                ln_phase(ya, 12, xn_ab)
                join_all()
        if "B1" in bstg:
            with nc.named_scope("B1"):
                proj_phase(xn_ab)
                join_all()
        if "B2" in bstg:
            with nc.named_scope("B2"):
                attn_phase()
                attn_debug_store()
                join_all()
        if "B3" in bstg:
            with nc.named_scope("B3"):
                oproj_phase()
                join_all()

        if "C" in phases:
            src_c = yb if "B3" in bstg else ya[0:T_OWN, :]
            with nc.named_scope("C0"):
                ln_phase(src_c, 8, xn_c)
                join_all()
            with nc.named_scope("C1"):
                ffn_up(xn_c, 2)
                join_all()
            with nc.named_scope("C2"):
                ffn_down(src_c, out, 2)
                join_all()

    return nc



def _prep_weights(wq, wk, wv, wo, wgate, wup, wdown):
    def blob(wT, n_out, chunk):
        return np.ascontiguousarray(
            wT.reshape(KC, 128, n_out // chunk, chunk).transpose(2, 1, 0, 3))

    wg_t = blob(np.ascontiguousarray(wgate.T), INTER, 256)
    wu_t = blob(np.ascontiguousarray(wup.T), INTER, 256)
    wdT = np.ascontiguousarray(wdown.T)
    wd_t = np.ascontiguousarray(wdT.reshape(86, 128, 8, 512).transpose(2, 1, 0, 3))
    wk_t = blob(np.ascontiguousarray(wk.T), NKV * HD, 128)
    wv_t = blob(np.ascontiguousarray(wv.T), NKV * HD, 128)
    wq_t = blob(np.ascontiguousarray(wq.T), NQ * HD, 128)
    wo_t = blob(np.ascontiguousarray(wo.T), D, 256)
    return dict(wg_t=wg_t, wu_t=wu_t, wd_t=wd_t, wk_t=wk_t, wv_t=wv_t,
                wq_t=wq_t, wo_t=wo_t)


def _masks(core):
    m = np.zeros((8, 128, 640), np.float32)
    j = np.arange(640)[None, :]
    qr = np.arange(128)[:, None]
    band = (j > qr) & (j <= qr + WIN)
    for qt in range(8):
        valid = band
        if core == 0:
            valid = valid & ((qt * 128 + j) >= HALO)
        m[qt] = np.where(valid, 0.0, NEG)
    return m


def kernel(L0_attn_x, L0_attn_wq, L0_attn_wk, L0_attn_wv, L0_attn_wo,
           L0_ffn_w_gate, L0_ffn_w_up, L0_ffn_w_down):
    x = np.asarray(L0_attn_x, np.float32).reshape(8192, D)
    wb = _prep_weights(np.asarray(L0_attn_wq, np.float32),
                       np.asarray(L0_attn_wk, np.float32),
                       np.asarray(L0_attn_wv, np.float32),
                       np.asarray(L0_attn_wo, np.float32),
                       np.asarray(L0_ffn_w_gate, np.float32),
                       np.asarray(L0_ffn_w_up, np.float32),
                       np.asarray(L0_ffn_w_down, np.float32))
    ident = np.eye(128, dtype=np.float32)

    nc = build_nc()
    in_maps = []
    for c in range(8):
        m = dict(wb)
        m["x_in"] = np.ascontiguousarray(x[c * T_OWN: (c + 1) * T_OWN])
        m["mask_in"] = _masks(c)
        m["ident_in"] = ident
        in_maps.append(m)
    res = run_bass_kernel_spmd(nc, in_maps, core_ids=list(range(8)))
    outp = np.concatenate([res.results[c]["out"] for c in range(8)], axis=0)
    return outp.reshape(1, 8192, D)


# version 08
# speedup vs baseline: 1.1078x; 1.0010x over previous
import os
import numpy as np
import ml_dtypes
import concourse.bass as bass
import concourse.mybir as mybir
from concourse.bass_utils import run_bass_kernel_spmd

F32 = mybir.dt.float32
BF16 = mybir.dt.bfloat16
U8 = mybir.dt.uint8
AF = mybir.ActivationFunctionType
AX = mybir.AxisListType
OP = mybir.AluOpType

D = 4096
KC = 32
INTER = 11008
NIC = 43
T_A = 1536
T_OWN = 1024
HALO = 512
NQ, NKV, HD, WIN = 32, 8, 128, 512
QK_SCALE = float(1.0 / np.sqrt(128.0))
NEG = -1.0e30
KB = 1024
f32c = np.float32
SEM_NAMES = {}


def _carve(arena, off, shape, dtype):
    nb = int(np.prod(shape[1:])) * mybir.dt.size(dtype)
    assert off % 32 == 0
    v = arena[:, off:off + nb].bitcast(dtype)
    if len(shape) > 2:
        dims = " ".join(f"d{i}" for i in range(1, len(shape)))
        v = v.rearrange(f"p ({dims}) -> p {dims}",
                        **{f"d{i}": shape[i] for i in range(1, len(shape))})
    return v, off + ((nb + 31) // 32) * 32


class Ctr:
    def __init__(self, h):
        self.h = h
        self.n = 0

    def inc(self, ins, k=1):
        ins.then_inc(self.h, k)
        self.n += k
        return self.n


def build_nc(phases=("A", "B", "C"), expose_ya=False, expose_yb=False,
             expose_q=False, expose_o=False):
    nc = bass.Bass("TRN2", target_bir_lowering=False)

    x_in = nc.dram_tensor("x_in", [T_OWN, D], F32, kind="ExternalInput").ap()
    wg_t = nc.dram_tensor("wg_t", [NIC, 128, KC, 256], F32, kind="ExternalInput").ap()
    wu_t = nc.dram_tensor("wu_t", [NIC, 128, KC, 256], F32, kind="ExternalInput").ap()
    wd_t = nc.dram_tensor("wd_t", [8, 128, 86, 512], F32, kind="ExternalInput").ap()
    wk_t = nc.dram_tensor("wk_t", [NKV, 128, KC, 128], F32, kind="ExternalInput").ap()
    wv_t = nc.dram_tensor("wv_t", [NKV, 128, KC, 128], F32, kind="ExternalInput").ap()
    wq_t = nc.dram_tensor("wq_t", [NQ, 128, KC, 128], F32, kind="ExternalInput").ap()
    wo_t = nc.dram_tensor("wo_t", [16, 128, KC, 256], F32, kind="ExternalInput").ap()
    mask_in = nc.dram_tensor("mask_in", [8, 128, 640], F32, kind="ExternalInput").ap()
    ident_in = nc.dram_tensor("ident_in", [128, 128], F32, kind="ExternalInput").ap()

    out = nc.dram_tensor("out", [T_OWN, D], F32, kind="ExternalOutput").ap()
    ya = nc.dram_tensor("ya", [T_A, D], F32,
                        kind="ExternalOutput" if expose_ya else "Internal").ap()
    yb = nc.dram_tensor("yb", [T_OWN, D], F32,
                        kind="ExternalOutput" if expose_yb else "Internal").ap()
    hbuf = nc.dram_tensor("hbuf", [86, 128, T_A], BF16, kind="Internal").ap()
    cc_in = nc.dram_tensor("cc_in", [HALO, D], F32, kind="Internal").ap()
    cc_out = nc.dram_tensor("cc_out", [8 * HALO, D], F32, kind="Internal",
                        addr_space="Shared").ap()
    qbuf = nc.dram_tensor("qbuf", [NQ, 128, T_OWN], BF16,
                          kind="ExternalOutput" if expose_q else "Internal").ap()
    obuf = (nc.dram_tensor("obuf", [NQ, 128, T_OWN], BF16,
                           kind="ExternalOutput").ap() if expose_o else None)

    from contextlib import ExitStack
    ctx = ExitStack()
    with ctx:
        arena = ctx.enter_context(nc.sbuf_tensor("arena", [128, 192 * KB], U8)).ap()
        bank = [ctx.enter_context(nc.psum_tensor(f"bank{i}", [128, 512], F32)).ap()
                for i in range(8)]

        o = 0
        idb, o = _carve(arena, o, [128, 128], BF16)
        idf, o = _carve(arena, o, [128, 128], F32)
        stats, o = _carve(arena, o, [128, 32], F32)
        assert o <= 1 * KB
        RAWS = stats[:, 0:1]
        MEAN = stats[:, 1:2]
        SUMSQ = stats[:, 2:3]
        VAR = stats[:, 3:4]
        M2 = stats[:, 4:5]
        RSTD_S = stats[:, 5:6]
        RSTD = stats[:, 6:7]
        MNEG = [stats[:, 8 + 4 * i:9 + 4 * i] for i in range(4)]
        SSUM = [stats[:, 9 + 4 * i:10 + 4 * i] for i in range(4)]
        RINV = [stats[:, 10 + 4 * i:11 + 4 * i] for i in range(4)]

        xn_ab, _ = _carve(arena, 1 * KB, [128, KC, T_A], BF16)
        xn_c, _ = _carve(arena, 1 * KB, [128, KC, T_OWN], BF16)
        wd_cache, _ = _carve(arena, 1 * KB, [128, 86, 512], BF16)
        o2 = 1 * KB
        qg_b = []
        for i in range(2):
            v, o2 = _carve(arena, o2, [128, 4, T_OWN], BF16)
            qg_b.append(v)
        obuf_res, o2 = _carve(arena, o2, [128, NQ, T_OWN], BF16)
        s_sb, p_sb, pn_sb = [], [], []
        for i in range(4):
            v, o2 = _carve(arena, o2, [128, 640], F32)
            s_sb.append(v)
        assert o2 <= 97 * KB, o2
        o2w = 145 * KB
        for i in range(4):
            v, o2w = _carve(arena, o2w, [128, 640], F32)
            p_sb.append(v)
        pt_sb, o2w = _carve(arena, o2w, [128, 5, 4, 128], BF16)
        assert o2w <= 161 * KB, o2w
        o2s = 161 * KB
        mask_sb, o2s = _carve(arena, o2s, [128, 8, 640], F32)
        for i in range(8):
            v, o2s = _carve(arena, o2s, [128, 640], BF16)
            pn_sb.append(v)
        assert o2s <= 192 * KB, o2s
        o3 = 97 * KB
        wo_b = []
        for i in range(2):
            v, o3 = _carve(arena, o3, [128, KC, 256], BF16)
            wo_b.append(v)
        y3_b = []
        for i in range(4):
            v, o3 = _carve(arena, o3, [128, 2, 512], F32)
            y3_b.append(v)
        r3_b = []
        for i in range(6):
            v, o3 = _carve(arena, o3, [128, 4, 256], F32)
            r3_b.append(v)
        os3_b = []
        for i in range(2):
            v, o3 = _carve(arena, o3, [128, 4, 256], F32)
            os3_b.append(v)
        assert o3 <= 184 * KB, o3

        W1 = 97 * KB
        o4 = W1
        wgb, wub = [], []
        for i in range(2):
            v, o4 = _carve(arena, o4, [128, KC, 256], BF16)
            wgb.append(v)
        for i in range(2):
            v, o4 = _carve(arena, o4, [128, KC, 256], BF16)
            wub.append(v)
        assert o4 <= 161 * KB
        o5 = W1
        trash, o5 = _carve(arena, o5, [128, D], F32)
        xt_b, xnb_b = [], []
        for i in range(2):
            v, o5 = _carve(arena, o5, [128, D], F32)
            xt_b.append(v)
        for i in range(2):
            v, o5 = _carve(arena, o5, [128, D], BF16)
            xnb_b.append(v)
        assert o5 <= 161 * KB
        o6 = W1
        hch_b = []
        for i in range(2):
            row = []
            for tc in range(2):
                v, o6 = _carve(arena, o6, [128, 8, 512], BF16)
                row.append(v)
            hch_b.append(row)
        y2_b = []
        for i in range(4):
            v, o6 = _carve(arena, o6, [128, 4, 512], F32)
            y2_b.append(v)
        assert o6 <= 161 * KB
        o6x = 1 * KB
        wdch_b = []
        for i in range(2):
            v, o6x = _carve(arena, o6x, [128, 8, 512], BF16)
            wdch_b.append(v)
        r2_b, os2_b = [], []
        for i in range(4):
            v, o6x = _carve(arena, o6x, [128, 4, 512], F32)
            r2_b.append(v)
        for i in range(2):
            v, o6x = _carve(arena, o6x, [128, 4, 512], F32)
            os2_b.append(v)
        assert o6x <= 97 * KB
        o7 = W1
        k_sb, o7 = _carve(arena, o7, [128, NKV, T_A], BF16)
        v_tok, o7 = _carve(arena, o7, [128, NKV, 12, 128], BF16)
        wpj_b = []
        for i in range(2):
            v, o7 = _carve(arena, o7, [128, KC, 128], BF16)
            wpj_b.append(v)
        assert o7 <= 161 * KB
        o9 = 161 * KB
        vf_sb, o9 = _carve(arena, o9, [128, T_A], BF16)
        qs_b = []
        for i in range(2):
            v, o9 = _carve(arena, o9, [128, 512], BF16)
            qs_b.append(v)
        gs_sb, o9 = _carve(arena, o9, [128, 3, 512], F32)
        hs_b = []
        for i in range(2):
            v, o9 = _carve(arena, o9, [128, 3, 512], BF16)
            hs_b.append(v)
        assert o9 <= 180 * KB, o9

        block = ctx.enter_context(nc.Block())

        s_wl = Ctr(ctx.enter_context(nc.semaphore("s_wl")))
        s_hl = Ctr(ctx.enter_context(nc.semaphore("s_hl")))
        s_st = Ctr(ctx.enter_context(nc.semaphore("s_st")))
        s_pe = Ctr(ctx.enter_context(nc.semaphore("s_pe")))
        s_ac = Ctr(ctx.enter_context(nc.semaphore("s_ac")))
        s_dv = Ctr(ctx.enter_context(nc.semaphore("s_dv")))
        ALL = [s_wl, s_hl, s_st, s_pe, s_ac, s_dv]
        global SEM_NAMES
        SEM_NAMES = {c.h.num: n for c, n in zip(ALL, ["wl", "hl", "st", "pe", "ac", "dv"])}

        def join_all():
            snap = [(c.h, c.n) for c in ALL]

            def wait_all(e):
                for h, n in snap:
                    if n:
                        e.wait_ge(h, n)
            block.sync(wait_all)
            block.gpsimd(wait_all)
            block.tensor(wait_all)
            block.scalar(wait_all)
            block.vector(wait_all)

        def p_pre_g(g):
            g.dma_start(idb[:], ident_in).then_inc(s_wl.h, 16)
        block.gpsimd(p_pre_g)
        s_wl.n += 16

        def p_pre_s(sy):
            sy.dma_start(idf[:], ident_in).then_inc(s_hl.h, 16)
        block.sync(p_pre_s)
        s_hl.n += 16
        join_all()

        def ln_phase(src, n_tiles, xn, order=None, gate=None):
            order = list(order) if order is not None else list(range(n_tiles))
            hl0, ac0, dv0, pe0 = s_hl.n, s_ac.n, s_dv.n, s_pe.n
            DVP, ACP, PEP = 6, 6, 4

            def f_sync(sy):
                gated = False
                for pos, tt in enumerate(order):
                    if gate is not None and not gated and tt in gate[2]:
                        sy.wait_ge(gate[0], gate[1])
                        gated = True
                    if pos >= 2:
                        sy.wait_ge(s_dv.h, dv0 + DVP * (pos - 2) + 6)
                        sy.wait_ge(s_ac.h, ac0 + ACP * (pos - 2) + 1)
                    sy.dma_start(xt_b[pos % 2][:], src[tt * 128:(tt + 1) * 128, :]
                                 ).then_inc(s_hl.h, 16)
            block.sync(f_sync)
            s_hl.n += 16 * n_tiles

            def f_dve(v):
                for pos, tt in enumerate(order):
                    x_t = xt_b[pos % 2]
                    v.wait_ge(s_hl.h, hl0 + 16 * (pos + 1))
                    if pos >= 1:
                        v.wait_ge(s_dv.h, dv0 + DVP * pos)
                    ins = v.reduce_sum(RAWS, x_t[:], axis=AX.X)
                    ins.then_inc(s_dv.h, 1)
                    v.wait_ge(s_dv.h, dv0 + DVP * pos + 1)
                    ins = v.tensor_scalar_mul(MEAN, RAWS, 1.0 / D)
                    ins.then_inc(s_dv.h, 1)
                    v.wait_ge(s_ac.h, ac0 + ACP * pos + 1)
                    v.tensor_scalar(VAR, SUMSQ, 1.0 / D, None, OP.mult)
                    v.wait_ge(s_dv.h, dv0 + DVP * pos + 2)
                    ins = v.tensor_mul(M2, MEAN, MEAN)
                    ins.then_inc(s_dv.h, 1)
                    v.wait_ge(s_dv.h, dv0 + DVP * pos + 3)
                    ins = v.tensor_tensor(VAR, VAR, M2, OP.subtract)
                    ins.then_inc(s_dv.h, 1)
                    v.wait_ge(s_ac.h, ac0 + ACP * pos + 2)
                    ins = v.reciprocal(RSTD, RSTD_S)
                    ins.then_inc(s_dv.h, 1)
                    if pos >= 2:
                        v.wait_ge(s_pe.h, pe0 + PEP * (pos - 2) + 4)
                    v.wait_ge(s_dv.h, dv0 + DVP * pos + 5)
                    ins = v.tensor_scalar(xnb_b[pos % 2][:], x_t[:], MEAN, RSTD,
                                          OP.subtract, OP.mult)
                    ins.then_inc(s_dv.h, 1)
            block.vector(f_dve)
            s_dv.n += DVP * n_tiles

            def f_act(sc):
                for pos, tt in enumerate(order):
                    x_t = xt_b[pos % 2]
                    sc.wait_ge(s_hl.h, hl0 + 16 * (pos + 1))
                    if pos >= 1:
                        sc.wait_ge(s_dv.h, dv0 + DVP * (pos - 1) + 4)
                    ins = sc.activation(trash[:], x_t[:], AF.Square,
                                        bias=0.0, scale=1.0, accum_out=SUMSQ)
                    ins.then_inc(s_ac.h, 1)
                    sc.wait_ge(s_dv.h, dv0 + DVP * pos + 4)
                    ins = sc.activation(RSTD_S, VAR, AF.Sqrt, bias=f32c(1e-5),
                                        scale=1.0)
                    ins.then_inc(s_ac.h, 1)
                    for b in range(4):
                        sc.wait_ge(s_pe.h, pe0 + PEP * pos + b + 1)
                        src_ps = bank[6 + b % 2][:].bitcast(BF16)[:, 0:1024].rearrange(
                            "p (a c) -> p a c", a=8)
                        ins = sc.copy(xn[:, b * 8:(b + 1) * 8, tt * 128:(tt + 1) * 128],
                                      src_ps)
                        ins.then_inc(s_ac.h, 1)
            block.scalar(f_act)
            s_ac.n += ACP * n_tiles

            def f_pe(t):
                for pos, tt in enumerate(order):
                    t.wait_ge(s_dv.h, dv0 + DVP * pos + 6)
                    for b in range(4):
                        gb = pos * 4 + b
                        if gb >= 2:
                            tp, bp = divmod(gb - 2, 4)
                            t.wait_ge(s_ac.h, ac0 + ACP * tp + 3 + bp)
                        dst = bank[6 + b % 2][:].bitcast(BF16)
                        for j in range(8):
                            kc = b * 8 + j
                            ins = t.transpose(dst[:, j * 128:(j + 1) * 128],
                                              xnb_b[pos % 2][:, kc * 128:(kc + 1) * 128],
                                              idb[:])
                        ins.then_inc(s_pe.h, 1)
            block.tensor(f_pe)
            s_pe.n += PEP * n_tiles

        def ffn_up(xn, n_tc):
            wl0, pe0, ac0, dv0, st0 = s_wl.n, s_pe.n, s_ac.n, s_dv.n, s_st.n
            NG = 2 * NIC

            def f_g(g):
                for ic in range(NIC):
                    if ic >= 2:
                        g.wait_ge(s_pe.h, pe0 + 4 * ic - 4)
                    g.dma_start(wgb[ic % 2][:], wg_t[ic]).then_inc(s_wl.h, 16)
                    g.dma_start(wub[ic % 2][:], wu_t[ic]).then_inc(s_wl.h, 16)
            block.gpsimd(f_g)
            s_wl.n += 32 * NIC

            def f_pe(t):
                for ic in range(NIC):
                    t.wait_ge(s_wl.h, wl0 + 32 * (ic + 1))
                    for i2 in range(2):
                        gi = 2 * ic + i2
                        if gi >= 1:
                            t.wait_ge(s_ac.h, ac0 + gi)
                        for kc in range(KC):
                            for tc in range(n_tc):
                                ins = t.matmul(bank[tc][:],
                                               wgb[ic % 2][:, kc, i2 * 128:(i2 + 1) * 128],
                                               xn[:, kc, tc * 512:(tc + 1) * 512],
                                               start=(kc == 0), stop=(kc == KC - 1))
                        ins.then_inc(s_pe.h, 1)
                        if gi >= 1:
                            t.wait_ge(s_dv.h, dv0 + gi)
                        for kc in range(KC):
                            for tc in range(n_tc):
                                ins = t.matmul(bank[3 + tc][:],
                                               wub[ic % 2][:, kc, i2 * 128:(i2 + 1) * 128],
                                               xn[:, kc, tc * 512:(tc + 1) * 512],
                                               start=(kc == 0), stop=(kc == KC - 1))
                        ins.then_inc(s_pe.h, 1)
            block.tensor(f_pe)
            s_pe.n += 2 * NG

            def f_act(sc):
                for gi in range(NG):
                    sc.wait_ge(s_pe.h, pe0 + 2 * gi + 1)
                    if gi >= 1:
                        sc.wait_ge(s_dv.h, dv0 + gi)
                    for tc in range(n_tc):
                        ins = sc.activation(gs_sb[:, tc, :], bank[tc][:], AF.Silu,
                                            bias=0.0, scale=1.0)
                    ins.then_inc(s_ac.h, 1)
            block.scalar(f_act)
            s_ac.n += NG

            def f_dve(v):
                for gi in range(NG):
                    v.wait_ge(s_pe.h, pe0 + 2 * gi + 2)
                    v.wait_ge(s_ac.h, ac0 + gi + 1)
                    if gi >= 2:
                        v.wait_ge(s_st.h, st0 + 16 * (gi - 1))
                    for tc in range(n_tc):
                        ins = v.scalar_tensor_tensor(hs_b[gi % 2][:, tc, :],
                                                     bank[3 + tc][:], 0.5,
                                                     gs_sb[:, tc, :],
                                                     OP.mult, OP.mult)
                    ins.then_inc(s_dv.h, 1)
            block.vector(f_dve)
            s_dv.n += NG

            def f_store(sy):
                for gi in range(NG):
                    sy.wait_ge(s_dv.h, dv0 + gi + 1)
                    sy.dma_start(hbuf[gi, :, 0:n_tc * 512],
                                 hs_b[gi % 2][:, 0:n_tc, :].rearrange("p a c -> p (a c)")
                                 ).then_inc(s_st.h, 16)
            block.sync(f_store)
            s_st.n += 16 * NG

        def ffn_down(res_src, dst, n_tc):
            assert n_tc == 2
            wl0, pe0, ac0, dv0, hl0, st0 = (s_wl.n, s_pe.n, s_ac.n, s_dv.n,
                                            s_hl.n, s_st.n)
            NB = 11
            NT = 16

            def pe_oct(dmc, j):
                return pe0 + 13 * dmc + j + 1

            def pe_mm(dmc):
                return pe_oct(dmc, NB - 1)

            def pe_tr(g2):
                return pe0 + 13 * (g2 // 2) + NB + (g2 % 2) + 1

            def f_g(g):
                for dmc in range(8):
                    if dmc >= 1:
                        for g2 in range(2 * (dmc - 1), 2 * dmc):
                            g.wait_ge(s_dv.h, dv0 + 16 * (g2 + 1))
                            g.dma_start(
                                dst[(g2 % 2) * 512:((g2 % 2) + 1) * 512,
                                    (g2 // 2) * 512:((g2 // 2) + 1) * 512]
                                .rearrange("(a p) c -> p a c", p=128),
                                os2_b[g2 % 2][:]).then_inc(s_st.h, 16)
                    for j in range(NB):
                        go = dmc * NB + j
                        if go >= 2:
                            g.wait_ge(s_pe.h, pe_oct((go - 2) // NB, (go - 2) % NB))
                        nic8 = 8 if j < 10 else 6
                        g.dma_start(wdch_b[go % 2][:, 0:nic8, :],
                                    wd_t[dmc][:, j * 8:j * 8 + nic8, :]
                                    ).then_inc(s_wl.h, 16)
                for g2 in range(14, 16):
                    g.wait_ge(s_dv.h, dv0 + 16 * (g2 + 1))
                    g.dma_start(
                        dst[(g2 % 2) * 512:((g2 % 2) + 1) * 512,
                            (g2 // 2) * 512:((g2 // 2) + 1) * 512]
                        .rearrange("(a p) c -> p a c", p=128),
                        os2_b[g2 % 2][:]).then_inc(s_st.h, 16)
            block.gpsimd(f_g)
            s_wl.n += 16 * 8 * NB
            s_st.n += 16 * NT

            def hl_at(dmc, j, tc):
                return hl0 + 16 * (24 * dmc + 2 + 2 * j + tc + 1)

            def f_load(sy):
                for dmc in range(8):
                    for tc in range(2):
                        g2 = 2 * dmc + tc
                        if g2 >= 4:
                            sy.wait_ge(s_dv.h, dv0 + 16 * (g2 - 3))
                        sy.dma_start(
                            r2_b[g2 % 4][:],
                            res_src[tc * 512:(tc + 1) * 512,
                                    dmc * 512:(dmc + 1) * 512]
                            .rearrange("(a p) c -> p a c", p=128)
                            ).then_inc(s_hl.h, 16)
                    for j in range(NB):
                        go = dmc * NB + j
                        if go >= 2:
                            sy.wait_ge(s_pe.h, pe_oct((go - 2) // NB, (go - 2) % NB))
                        nic8 = 8 if j < 10 else 6
                        for tc in range(2):
                            sy.dma_start(
                                hch_b[go % 2][tc][:, 0:nic8, :],
                                hbuf[j * 8:j * 8 + nic8, :, tc * 512:(tc + 1) * 512]
                                .rearrange("a p c -> p a c")).then_inc(s_hl.h, 16)
            block.sync(f_load)
            s_hl.n += 16 * 8 * 24

            def f_pe(t):
                for dmc in range(8):
                    if dmc >= 1:
                        t.wait_ge(s_dv.h, dv0 + 16 * 2 * dmc)
                    for j in range(NB):
                        go = dmc * NB + j
                        t.wait_ge(s_wl.h, wl0 + 16 * (go + 1))
                        t.wait_ge(s_hl.h, hl_at(dmc, j, 1))
                        nic8 = 8 if j < 10 else 6
                        for i8 in range(nic8):
                            ic = j * 8 + i8
                            for dm2 in range(4):
                                for tc in range(2):
                                    ins = t.matmul(
                                        bank[dm2 * 2 + tc][:],
                                        wdch_b[go % 2][:, i8, dm2 * 128:(dm2 + 1) * 128],
                                        hch_b[go % 2][tc][:, i8, :],
                                        start=(ic == 0), stop=(ic == 85))
                        ins.then_inc(s_pe.h, 1)
                    for tc in range(2):
                        g2 = 2 * dmc + tc
                        t.wait_ge(s_ac.h, ac0 + g2 + 1)
                        for tt in range(4):
                            for dm2 in range(4):
                                ins = t.transpose(
                                    bank[dm2 * 2 + tc][:, tt * 128:(tt + 1) * 128],
                                    y2_b[g2 % 4][:, dm2, tt * 128:(tt + 1) * 128],
                                    idf[:])
                        ins.then_inc(s_pe.h, 1)
            block.tensor(f_pe)
            s_pe.n += 13 * 8

            def f_act(sc):
                for dmc in range(8):
                    for tc in range(2):
                        g2 = 2 * dmc + tc
                        sc.wait_ge(s_pe.h, pe_mm(dmc))
                        if g2 >= 4:
                            sc.wait_ge(s_pe.h, pe_tr(g2 - 4))
                        for dm2 in range(4):
                            ins = sc.copy(y2_b[g2 % 4][:, dm2, :],
                                          bank[dm2 * 2 + tc][:])
                        ins.then_inc(s_ac.h, 1)
            block.scalar(f_act)
            s_ac.n += NT

            def f_dve(v):
                for g2 in range(NT):
                    dmc, tc = divmod(g2, 2)
                    v.wait_ge(s_pe.h, pe_tr(g2))
                    v.wait_ge(s_hl.h, hl0 + 16 * (24 * dmc + tc + 1))
                    if g2 >= 2:
                        v.wait_ge(s_st.h, st0 + 16 * (g2 - 1))
                    for tt in range(4):
                        for dm2 in range(4):
                            ins = v.tensor_tensor(
                                os2_b[g2 % 2][:, tt, dm2 * 128:(dm2 + 1) * 128],
                                bank[dm2 * 2 + tc][:, tt * 128:(tt + 1) * 128],
                                r2_b[g2 % 4][:, tt, dm2 * 128:(dm2 + 1) * 128],
                                OP.add)
                    ins.then_inc(s_dv.h, 16)
            block.vector(f_dve)
            s_dv.n += 16 * NT

        def proj_phase(xn):
            wl0, pe0, ac0, dv0, st0 = s_wl.n, s_pe.n, s_ac.n, s_dv.n, s_st.n

            def pe_k(kh):
                return pe0 + kh + 1

            def pe_vp(kh):
                return pe0 + NKV + 2 * kh + 1

            def pe_vt(kh):
                return pe0 + NKV + 2 * kh + 2

            def pe_q(h):
                return pe0 + 3 * NKV + h + 1

            def ac_k(kh):
                return ac0 + kh + 1

            def ac_vf(kh):
                return ac0 + NKV + 3 * kh + 1

            def ac_vt(kh, i):
                return ac0 + NKV + 3 * kh + 2 + i

            def ac_q(h, tc):
                return ac0 + 4 * NKV + 2 * h + tc + 1

            def f_w(g):
                for kh in range(NKV):
                    if kh >= 2:
                        g.wait_ge(s_pe.h, pe_k(kh - 2))
                    g.dma_start(wpj_b[kh % 2][:], wk_t[kh]).then_inc(s_wl.h, 16)
                for kh in range(NKV):
                    g.wait_ge(s_pe.h, pe_k(6 + kh % 2) if kh < 2
                              else pe_vt(kh - 2))
                    g.dma_start(wpj_b[kh % 2][:], wv_t[kh]).then_inc(s_wl.h, 16)
                for h in range(NQ):
                    g.wait_ge(s_pe.h, pe_vt(6 + h % 2) if h < 2
                              else pe_q(h - 2))
                    g.dma_start(wpj_b[h % 2][:], wq_t[h]).then_inc(s_wl.h, 16)
            block.gpsimd(f_w)
            s_wl.n += 16 * (2 * NKV + NQ)

            def f_pe(t):
                for kh in range(NKV):
                    t.wait_ge(s_wl.h, wl0 + 16 * (kh + 1))
                    if kh >= 1:
                        t.wait_ge(s_ac.h, ac_k(kh - 1))
                    for kc in range(KC):
                        for tc in range(3):
                            ins = t.matmul(bank[tc][:],
                                           wpj_b[kh % 2][:, kc, :],
                                           xn[:, kc, tc * 512:(tc + 1) * 512],
                                           start=(kc == 0), stop=(kc == KC - 1))
                    ins.then_inc(s_pe.h, 1)
                for kh in range(NKV):
                    t.wait_ge(s_wl.h, wl0 + 16 * NKV + 16 * (kh + 1))
                    t.wait_ge(s_ac.h, ac_vf(kh - 1) if kh >= 1 else ac_k(NKV - 1))
                    for kc in range(KC):
                        for tc in range(3):
                            ins = t.matmul(bank[tc][:],
                                           wpj_b[kh % 2][:, kc, :],
                                           xn[:, kc, tc * 512:(tc + 1) * 512],
                                           start=(kc == 0), stop=(kc == KC - 1))
                    ins.then_inc(s_pe.h, 1)
                    t.wait_ge(s_ac.h, ac_vf(kh))
                    if kh >= 1:
                        t.wait_ge(s_ac.h, ac_vt(kh - 1, 1))
                    for tt in range(12):
                        dstb = bank[3 + tt // 8][:].bitcast(BF16)
                        ins = t.transpose(dstb[:, (tt % 8) * 128:(tt % 8 + 1) * 128],
                                          vf_sb[:, tt * 128:(tt + 1) * 128], idb[:])
                    ins.then_inc(s_pe.h, 1)
                for h in range(NQ):
                    t.wait_ge(s_wl.h, wl0 + 32 * NKV + 16 * (h + 1))
                    if h >= 1:
                        t.wait_ge(s_ac.h, ac_q(h - 1, 1))
                    for kc in range(KC):
                        for tc in range(2):
                            ins = t.matmul(bank[5 + tc][:],
                                           wpj_b[h % 2][:, kc, :],
                                           xn[:, kc, HALO + tc * 512:HALO + (tc + 1) * 512],
                                           start=(kc == 0), stop=(kc == KC - 1))
                    ins.then_inc(s_pe.h, 1)
            block.tensor(f_pe)
            s_pe.n += 3 * NKV + NQ

            def f_act(sc):
                for kh in range(NKV):
                    sc.wait_ge(s_pe.h, pe_k(kh))
                    for tc in range(3):
                        ins = sc.copy(k_sb[:, kh, tc * 512:(tc + 1) * 512],
                                      bank[tc][:])
                    ins.then_inc(s_ac.h, 1)
                for kh in range(NKV):
                    sc.wait_ge(s_pe.h, pe_vp(kh))
                    if kh >= 1:
                        sc.wait_ge(s_pe.h, pe_vt(kh - 1))
                    for tc in range(3):
                        ins = sc.copy(vf_sb[:, tc * 512:(tc + 1) * 512], bank[tc][:])
                    ins.then_inc(s_ac.h, 1)
                    sc.wait_ge(s_pe.h, pe_vt(kh))
                    ins = sc.copy(v_tok[:, kh, 0:8, :],
                                  bank[3][:].bitcast(BF16)[:, 0:1024].rearrange(
                                      "p (a c) -> p a c", a=8))
                    ins.then_inc(s_ac.h, 1)
                    ins = sc.copy(v_tok[:, kh, 8:12, :],
                                  bank[4][:].bitcast(BF16)[:, 0:512].rearrange(
                                      "p (a c) -> p a c", a=4))
                    ins.then_inc(s_ac.h, 1)
                for h in range(NQ):
                    sc.wait_ge(s_pe.h, pe_q(h))
                    for tc in range(2):
                        qi = 2 * h + tc
                        if qi >= 2:
                            sc.wait_ge(s_st.h, st0 + 16 * (qi - 1))
                        ins = sc.activation(qs_b[qi % 2][:], bank[5 + tc][:],
                                            AF.Copy, bias=0.0, scale=f32c(QK_SCALE))
                        ins.then_inc(s_ac.h, 1)
            block.scalar(f_act)
            s_ac.n += 4 * NKV + 2 * NQ

            def f_qstore(sy):
                for h in range(NQ):
                    for tc in range(2):
                        sy.wait_ge(s_ac.h, ac_q(h, tc))
                        sy.dma_start(qbuf[h, :, tc * 512:(tc + 1) * 512],
                                     qs_b[(2 * h + tc) % 2][:]).then_inc(s_st.h, 16)
            block.sync(f_qstore)
            s_st.n += 16 * 2 * NQ

        def attn_phase():
            pe0, ac0, dv0, hl0, st0 = s_pe.n, s_ac.n, s_dv.n, s_hl.n, s_st.n
            PEQ, ACQ, DVQ = 9, 9, 4

            NGQ = 8 * NKV

            def pe_s(gq, h4):
                return pe0 + 4 * gq + 5 * max(0, gq - 1) + h4 + 1

            def pe_pt(gh):
                gq, h4 = divmod(gh, 4)
                ns = 4 * min(gq + 2, NGQ)
                return pe0 + ns + 5 * gq + h4 + 1

            def pe_av(gq):
                ns = 4 * min(gq + 2, NGQ)
                return pe0 + ns + 5 * gq + 5

            def ac_exp(gh):
                gq = gh // 4
                return ac0 + 4 * gq + 5 * max(0, gq - 1) + gh % 4 + 1

            def ac_pt(gh):
                gq, h4 = divmod(gh, 4)
                ns = 4 * min(gq + 2, 8 * NKV)
                return ac0 + ns + 5 * gq + h4 + 1

            def ac_oev(gq):
                ns = 4 * min(gq + 2, 8 * NKV)
                return ac0 + ns + 5 * gq + 5

            def dv_mask(gh):
                gq, h4 = divmod(gh, 4)
                return dv0 + 16 * gq + 2 * h4 + 1

            def dv_max(gh):
                gq, h4 = divmod(gh, 4)
                return dv0 + 16 * gq + 2 * h4 + 2

            def dv_rec(gh):
                gq, h4 = divmod(gh, 4)
                return dv0 + 16 * gq + 8 + 2 * h4 + 1

            def dv_pn(gh):
                gq, h4 = divmod(gh, 4)
                return dv0 + 16 * gq + 8 + 2 * h4 + 2

            def f_qload(sy):
                sy.dma_start(mask_sb[:], mask_in.rearrange("q p k -> p q k")
                             ).then_inc(s_hl.h, 16)
                for kh in range(NKV):
                    if kh >= 2:
                        sy.wait_ge(s_pe.h, pe_s(8 * (kh - 2) + 7, 3))
                    sy.dma_start(qg_b[kh % 2][:],
                                 qbuf[4 * kh:4 * kh + 4].rearrange("h p q -> p h q")
                                 ).then_inc(s_hl.h, 16)
            block.sync(f_qload)
            s_hl.n += 16 * (NKV + 1)

            def f_pe(t):
                def s_block(gq):
                    kh, qt = divmod(gq, 8)
                    if qt == 0:
                        t.wait_ge(s_hl.h, hl0 + 16 * (kh + 2))
                    for h4 in range(4):
                        gh = 4 * gq + h4
                        par = h4 % 2
                        if gh >= 2:
                            t.wait_ge(s_dv.h, dv_mask(gh - 2))
                        ins = t.matmul(bank[2 * par][:],
                                       qg_b[kh % 2][:, h4, qt * 128:(qt + 1) * 128],
                                       k_sb[:, kh, qt * 128:qt * 128 + 512],
                                       start=True, stop=True)
                        ins2 = t.matmul(bank[2 * par + 1][:, 0:128],
                                        qg_b[kh % 2][:, h4, qt * 128:(qt + 1) * 128],
                                        k_sb[:, kh, qt * 128 + 512:qt * 128 + 640],
                                        start=True, stop=True)
                        ins2.then_inc(s_pe.h, 1)

                def ptav_block(gq):
                    kh, qt = divmod(gq, 8)
                    for h4 in range(4):
                        gh = 4 * gq + h4
                        t.wait_ge(s_dv.h, dv_pn(gh))
                        if gh >= 2:
                            t.wait_ge(s_ac.h, ac_pt(gh - 2))
                        dstb = bank[4 + h4 % 2][:].bitcast(BF16)
                        for j in range(5):
                            ins = t.transpose(dstb[:, j * 128:(j + 1) * 128],
                                              pn_sb[gh % 8][:, j * 128:(j + 1) * 128],
                                              idb[:])
                        ins.then_inc(s_pe.h, 1)
                    t.wait_ge(s_ac.h, ac_pt(4 * gq + 3))
                    ob = 6 + qt % 2
                    if gq >= 2:
                        t.wait_ge(s_ac.h, ac_oev(gq - 2))
                    for j in range(5):
                        ins = t.matmul(bank[ob][:],
                                       v_tok[:, kh, qt + j, :],
                                       pt_sb[:, j, :, :].rearrange(
                                           "p a c -> p (a c)"),
                                       start=(j == 0), stop=(j == 4))
                    ins.then_inc(s_pe.h, 1)

                for gq in range(8 * NKV):
                    s_block(gq)
                    if gq >= 1:
                        ptav_block(gq - 1)
                ptav_block(8 * NKV - 1)
            block.tensor(f_pe)
            s_pe.n += PEQ * 8 * NKV


            def f_dve(v):
                for gq in range(8 * NKV):
                    kh, qt = divmod(gq, 8)
                    for h4 in range(4):
                        gh = 4 * gq + h4
                        par = h4
                        sb2 = h4 % 2
                        v.wait_ge(s_pe.h, pe_s(gq, h4))
                        if gh >= 4:
                            v.wait_ge(s_ac.h, ac_exp(gh - 4))
                        v.tensor_tensor(s_sb[par][:, 0:512], bank[2 * sb2][:],
                                        mask_sb[:, qt, 0:512], OP.add)
                        ins = v.tensor_tensor(s_sb[par][:, 512:640],
                                              bank[2 * sb2 + 1][:, 0:128],
                                              mask_sb[:, qt, 512:640], OP.add)
                        ins.then_inc(s_dv.h, 1)
                        v.wait_ge(s_dv.h, dv_mask(gh))
                        ins = v.reduce_max(MNEG[par], s_sb[par][:], axis=AX.X,
                                           negate=True)
                        ins.then_inc(s_dv.h, 1)
                    for h4 in range(4):
                        gh = 4 * gq + h4
                        par = h4
                        v.wait_ge(s_ac.h, ac_exp(gh))
                        if gh >= 4:
                            v.wait_ge(s_dv.h, dv_pn(gh - 4))
                        ins = v.reciprocal(RINV[par], SSUM[par])
                        ins.then_inc(s_dv.h, 1)
                        v.wait_ge(s_dv.h, dv_rec(gh))
                        if gh >= 8:
                            v.wait_ge(s_pe.h, pe_pt(gh - 8))
                        ins = v.tensor_scalar(pn_sb[gh % 8][:], p_sb[par][:],
                                              RINV[par], None, OP.mult)
                        ins.then_inc(s_dv.h, 1)
            block.vector(f_dve)
            s_dv.n += DVQ * 4 * 8 * NKV

            def f_act(sc):
                def exp_block(gq):
                    for h4 in range(4):
                        gh = 4 * gq + h4
                        par = h4
                        sc.wait_ge(s_dv.h, dv_max(gh))
                        if gh >= 4:
                            sc.wait_ge(s_dv.h, dv_pn(gh - 4))
                        ins = sc.activation(p_sb[par][:], s_sb[par][:], AF.Exp,
                                            bias=MNEG[par], scale=1.0,
                                            accum_out=SSUM[par])
                        ins.then_inc(s_ac.h, 1)

                def ptev_block(gq):
                    kh, qt = divmod(gq, 8)
                    for h4 in range(4):
                        gh = 4 * gq + h4
                        sc.wait_ge(s_pe.h, pe_pt(gh))
                        if gq >= 1:
                            sc.wait_ge(s_pe.h, pe_av(gq - 1))
                        ins = sc.copy(pt_sb[:, :, h4, :],
                                      bank[4 + h4 % 2][:].bitcast(BF16)[:, 0:640]
                                      .rearrange("p (a c) -> p a c", a=5))
                        ins.then_inc(s_ac.h, 1)
                    sc.wait_ge(s_pe.h, pe_av(gq))
                    ins = sc.copy(obuf_res[:, 4 * kh:4 * kh + 4, qt * 128:(qt + 1) * 128],
                                  bank[6 + qt % 2][:].rearrange("p (a c) -> p a c",
                                                                a=4))
                    ins.then_inc(s_ac.h, 1)

                for gq in range(8 * NKV):
                    exp_block(gq)
                    if gq >= 1:
                        ptev_block(gq - 1)
                ptev_block(8 * NKV - 1)
            block.scalar(f_act)
            s_ac.n += ACQ * 8 * NKV



        def attn_debug_store():
            if obuf is None:
                return
            snap_ac = s_ac.n

            def f_dbg(sy):
                sy.wait_ge(s_ac.h, snap_ac)
                sy.dma_start(obuf.rearrange("h p q -> p h q"), obuf_res[:]
                             ).then_inc(s_st.h, 16)
            block.sync(f_dbg)
            s_st.n += 16

        def oproj_phase():
            wl0, pe0, ac0, dv0, hl0, st0 = (s_wl.n, s_pe.n, s_ac.n, s_dv.n,
                                            s_hl.n, s_st.n)
            n_tc = 2
            NT = 16 * n_tc
            order = []
            for dmc in range(16):
                order.append(("mm", dmc))
                if dmc >= 1:
                    order += [("tr", (dmc - 1) * n_tc + tc) for tc in range(n_tc)]
            order += [("tr", 15 * n_tc + tc) for tc in range(n_tc)]
            pe_at = {ev: pe0 + i + 1 for i, ev in enumerate(order)}

            def st3(g, g2):
                g.wait_ge(s_dv.h, dv0 + g2 + 1)
                g.dma_start(
                    yb[(g2 % n_tc) * 512:((g2 % n_tc) + 1) * 512,
                       (g2 // n_tc) * 256:((g2 // n_tc) + 1) * 256]
                    .rearrange("(a p) c -> p a c", p=128),
                    os3_b[g2 % 2][:]).then_inc(s_st.h, 16)

            def f_g(g):
                for dmc in range(16):
                    if dmc >= 2:
                        for g2 in range((dmc - 2) * n_tc, (dmc - 1) * n_tc):
                            st3(g, g2)
                    if dmc >= 2:
                        g.wait_ge(s_pe.h, pe_at[("mm", dmc - 2)])
                    g.dma_start(wo_b[dmc % 2][:], wo_t[dmc]).then_inc(s_wl.h, 16)
                for g2 in range(14 * n_tc, 16 * n_tc):
                    st3(g, g2)
            block.gpsimd(f_g)
            s_wl.n += 16 * 16
            s_st.n += 16 * NT

            def f_load(sy):
                for g2 in range(NT):
                    dmc, tc = divmod(g2, n_tc)
                    if g2 >= 6:
                        sy.wait_ge(s_dv.h, dv0 + g2 - 5)
                    sy.dma_start(
                        r3_b[g2 % 6][:],
                        ya[HALO + tc * 512:HALO + (tc + 1) * 512,
                           dmc * 256:(dmc + 1) * 256]
                        .rearrange("(a p) c -> p a c", p=128)).then_inc(s_hl.h, 16)
            block.sync(f_load)
            s_hl.n += 16 * NT

            def f_pe(t):
                for dmc in range(16):
                    t.wait_ge(s_wl.h, wl0 + 16 * (dmc + 1))
                    if dmc >= 1:
                        t.wait_ge(s_ac.h, ac0 + n_tc * dmc)
                    for fc in range(KC):
                        for dm2 in range(2):
                            for tc in range(n_tc):
                                ins = t.matmul(
                                    bank[dm2 * n_tc + tc][:],
                                    wo_b[dmc % 2][:, fc, dm2 * 128:(dm2 + 1) * 128],
                                    obuf_res[:, fc, tc * 512:(tc + 1) * 512],
                                    start=(fc == 0), stop=(fc == KC - 1))
                    ins.then_inc(s_pe.h, 1)
                    if dmc >= 1:
                        for tc in range(n_tc):
                            g2 = (dmc - 1) * n_tc + tc
                            t.wait_ge(s_ac.h, ac0 + g2 + 1)
                            if g2 >= 1:
                                t.wait_ge(s_dv.h, dv0 + g2)
                            for tt in range(4):
                                for dm2 in range(2):
                                    ins = t.transpose(
                                        bank[6 + dm2][:, tt * 128:(tt + 1) * 128],
                                        y3_b[g2 % 4][:, dm2, tt * 128:(tt + 1) * 128],
                                        idf[:])
                            ins.then_inc(s_pe.h, 1)
                for tc in range(n_tc):
                    g2 = 15 * n_tc + tc
                    t.wait_ge(s_ac.h, ac0 + g2 + 1)
                    t.wait_ge(s_dv.h, dv0 + g2)
                    for tt in range(4):
                        for dm2 in range(2):
                            ins = t.transpose(
                                bank[6 + dm2][:, tt * 128:(tt + 1) * 128],
                                y3_b[g2 % 4][:, dm2, tt * 128:(tt + 1) * 128],
                                idf[:])
                    ins.then_inc(s_pe.h, 1)
            block.tensor(f_pe)
            s_pe.n += len(order)

            def f_act(sc):
                for dmc in range(16):
                    sc.wait_ge(s_pe.h, pe_at[("mm", dmc)])
                    for tc in range(n_tc):
                        g2 = dmc * n_tc + tc
                        if g2 >= 4:
                            sc.wait_ge(s_pe.h, pe_at[("tr", g2 - 4)])
                        for dm2 in range(2):
                            ins = sc.copy(y3_b[g2 % 4][:, dm2, :],
                                          bank[dm2 * n_tc + tc][:])
                        ins.then_inc(s_ac.h, 1)
            block.scalar(f_act)
            s_ac.n += NT

            def f_dve(v):
                for g2 in range(NT):
                    v.wait_ge(s_pe.h, pe_at[("tr", g2)])
                    v.wait_ge(s_hl.h, hl0 + 16 * (g2 + 1))
                    if g2 >= 2:
                        v.wait_ge(s_st.h, st0 + 16 * (g2 - 1))
                    for tt in range(4):
                        for dm2 in range(2):
                            ins = v.tensor_tensor(
                                os3_b[g2 % 2][:, tt, dm2 * 128:(dm2 + 1) * 128],
                                bank[6 + dm2][:, tt * 128:(tt + 1) * 128],
                                r3_b[g2 % 6][:, tt, dm2 * 128:(dm2 + 1) * 128],
                                OP.add)
                    ins.then_inc(s_dv.h, 1)
            block.vector(f_dve)
            s_dv.n += NT

        if "A" in phases:
            with nc.named_scope("A0"):
                ln_phase(x_in, 8, xn_c)
                join_all()
            with nc.named_scope("A1"):
                ffn_up(xn_c, 2)
                join_all()
            with nc.named_scope("A2"):
                ffn_down(x_in, ya[HALO:HALO + T_OWN, :], 2)
                join_all()
            with nc.named_scope("AX"):
                def f_exch(g):
                    g.dma_start(cc_in, ya[T_OWN:T_OWN + HALO, :]).then_inc(s_hl.h, 16)
                    g.wait_ge(s_hl.h, s_hl.n + 16)
                    g.collective_compute(
                        "AllGather", OP.bypass,
                        replica_groups=[[0, 1, 2, 3, 4, 5, 6, 7]],
                        ins=[cc_in.opt()],
                        outs=[cc_out.opt()],
                    ).then_inc(s_wl.h)
                block.gpsimd(f_exch)
                s_hl.n += 16
                s_wl.n += 1

                def f_pick(sy):
                    sy.wait_ge(s_wl.h, s_wl.n)
                    pid = sy.partition_id()
                    idx = (pid + 7) % 8
                    cc3 = cc_out.rearrange("(a b) c -> a b c", b=HALO)
                    sy.dma_start(ya[0:HALO, :], cc3[idx]).then_inc(s_hl.h, 16)
                block.sync(f_pick)
                s_hl.n += 16
                halo_gate = (s_hl.h, s_hl.n)

        bstg = set()
        for ph in phases:
            if ph == "B":
                bstg |= {"B0", "B1", "B2", "B3"}
            else:
                bstg.add(ph)
        if "B0" in bstg:
            with nc.named_scope("B0"):
                if "A" in phases:
                    ln_phase(ya, 12, xn_ab,
                             order=[4, 5, 6, 7, 8, 9, 10, 11, 0, 1, 2, 3],
                             gate=(halo_gate[0], halo_gate[1], {0, 1, 2, 3}))
                else:
                    ln_phase(ya, 12, xn_ab)
                join_all()
        if "B1" in bstg:
            with nc.named_scope("B1"):
                proj_phase(xn_ab)
                join_all()
        if "B2" in bstg:
            with nc.named_scope("B2"):
                attn_phase()
                attn_debug_store()
                join_all()
        if "B3" in bstg:
            with nc.named_scope("B3"):
                oproj_phase()
                join_all()

        if "C" in phases:
            src_c = yb if "B3" in bstg else ya[0:T_OWN, :]
            with nc.named_scope("C0"):
                ln_phase(src_c, 8, xn_c)
                join_all()
            with nc.named_scope("C1"):
                ffn_up(xn_c, 2)
                join_all()
            with nc.named_scope("C2"):
                ffn_down(src_c, out, 2)
                join_all()

    return nc



def _prep_weights(wq, wk, wv, wo, wgate, wup, wdown):
    def blob(wT, n_out, chunk):
        return np.ascontiguousarray(
            wT.reshape(KC, 128, n_out // chunk, chunk).transpose(2, 1, 0, 3))

    wg_t = blob(np.ascontiguousarray(wgate.T), INTER, 256)
    wu_t = blob(np.ascontiguousarray(wup.T), INTER, 256)
    wdT = np.ascontiguousarray(wdown.T)
    wd_t = np.ascontiguousarray(wdT.reshape(86, 128, 8, 512).transpose(2, 1, 0, 3))
    wk_t = blob(np.ascontiguousarray(wk.T), NKV * HD, 128)
    wv_t = blob(np.ascontiguousarray(wv.T), NKV * HD, 128)
    wq_t = blob(np.ascontiguousarray(wq.T), NQ * HD, 128)
    wo_t = blob(np.ascontiguousarray(wo.T), D, 256)
    return dict(wg_t=wg_t, wu_t=wu_t, wd_t=wd_t, wk_t=wk_t, wv_t=wv_t,
                wq_t=wq_t, wo_t=wo_t)


def _masks(core):
    m = np.zeros((8, 128, 640), np.float32)
    j = np.arange(640)[None, :]
    qr = np.arange(128)[:, None]
    band = (j > qr) & (j <= qr + WIN)
    for qt in range(8):
        valid = band
        if core == 0:
            valid = valid & ((qt * 128 + j) >= HALO)
        m[qt] = np.where(valid, 0.0, NEG)
    return m


def kernel(L0_attn_x, L0_attn_wq, L0_attn_wk, L0_attn_wv, L0_attn_wo,
           L0_ffn_w_gate, L0_ffn_w_up, L0_ffn_w_down):
    x = np.asarray(L0_attn_x, np.float32).reshape(8192, D)
    wb = _prep_weights(np.asarray(L0_attn_wq, np.float32),
                       np.asarray(L0_attn_wk, np.float32),
                       np.asarray(L0_attn_wv, np.float32),
                       np.asarray(L0_attn_wo, np.float32),
                       np.asarray(L0_ffn_w_gate, np.float32),
                       np.asarray(L0_ffn_w_up, np.float32),
                       np.asarray(L0_ffn_w_down, np.float32))
    ident = np.eye(128, dtype=np.float32)

    nc = build_nc()
    in_maps = []
    for c in range(8):
        m = dict(wb)
        m["x_in"] = np.ascontiguousarray(x[c * T_OWN: (c + 1) * T_OWN])
        m["mask_in"] = _masks(c)
        m["ident_in"] = ident
        in_maps.append(m)
    res = run_bass_kernel_spmd(nc, in_maps, core_ids=list(range(8)))
    outp = np.concatenate([res.results[c]["out"] for c in range(8)], axis=0)
    return outp.reshape(1, 8192, D)
